# version 10
# baseline (speedup 1.0000x reference)
import numpy as np
import ml_dtypes
import concourse.bass as bass
import concourse.mybir as mybir
from concourse.bass_utils import run_bass_kernel_spmd

F32 = mybir.dt.float32
BF16 = mybir.dt.bfloat16
AF = mybir.ActivationFunctionType
ALU = mybir.AluOpType

D = 2048
NH = 16
HD = 128
NB = 2
S = 4096
INTER = 8192
NT = 1024
DC = 16
ICN = 64
CEXP = 28.0
SCALE = 1.0 / float(np.sqrt(128.0))
EPS = 1e-5
RDINV = 1.0 / float(D)

DEBUG = False


def build_program():
    nc = bass.Bass("TRN2", target_bir_lowering=False, detect_race_conditions=False)

    xT = nc.dram_tensor("xT", [D, NT], F32, kind="ExternalInput").ap()
    wq = nc.dram_tensor("wq", [16, 128, 2048], F32, kind="ExternalInput").ap()
    wk = nc.dram_tensor("wk", [16, 128, 2048], F32, kind="ExternalInput").ap()
    wv = nc.dram_tensor("wv", [16, 128, 2048], F32, kind="ExternalInput").ap()
    wo = nc.dram_tensor("wo", [16, 128, 2048], F32, kind="ExternalInput").ap()
    wgu = nc.dram_tensor("wgu", [128, 128, 2048], F32, kind="ExternalInput").ap()
    wd = nc.dram_tensor("wd", [16, 128, 8192], F32, kind="ExternalInput").ap()
    qb = nc.dram_tensor("qb", [128, 16], F32, kind="ExternalInput").ap()
    kb = nc.dram_tensor("kb", [128, 16], F32, kind="ExternalInput").ap()
    vb = nc.dram_tensor("vb", [1, 2048], F32, kind="ExternalInput").ap()
    ob = nc.dram_tensor("ob", [128, 16], F32, kind="ExternalInput").ap()
    masks = nc.dram_tensor("masks", [128, 4, 128], BF16, kind="ExternalInput").ap()
    outT = nc.dram_tensor("outT", [D, NT], F32, kind="ExternalOutput").ap()

    dbg = {}
    if DEBUG:
        dbg["xn1"] = nc.dram_tensor("d_xn1", [128, DC, NT], BF16, kind="ExternalOutput").ap()
        dbg["q"] = nc.dram_tensor("d_q", [128, DC, NT], BF16, kind="ExternalOutput").ap()
        dbg["kag"] = nc.dram_tensor("d_kag", [4, 2048, 1024], BF16, kind="ExternalOutput").ap()
        dbg["vag"] = nc.dram_tensor("d_vag", [4, 1024, 2048], BF16, kind="ExternalOutput").ap()
        dbg["av"] = nc.dram_tensor("d_av", [128, DC, NT], BF16, kind="ExternalOutput").ap()
        dbg["mix"] = nc.dram_tensor("d_mix", [128, DC, NT], F32, kind="ExternalOutput").ap()
        dbg["xn2"] = nc.dram_tensor("d_xn2", [128, DC, NT], BF16, kind="ExternalOutput").ap()
        dbg["h"] = nc.dram_tensor("d_h", [128, ICN, NT], BF16, kind="ExternalOutput").ap()
        dbg["ffn"] = nc.dram_tensor("d_ffn", [2, 128, DC, 512], F32, kind="ExternalOutput").ap()
        dbg["kin"] = nc.dram_tensor("d_kin", [2048, 1024], BF16, kind="ExternalOutput").ap()

    cag_in = nc.dram_tensor("cag_in", [4096, 1024], BF16).ap()
    cag_out = nc.dram_tensor("cag_out", [4, 4096, 1024], BF16).ap()
    kag_in = cag_in[0:2048, :]
    vag_in = cag_in[2048:4096, :].rearrange("(t two) c -> t (two c)", two=2)
    kag_out = cag_out[:, 0:2048, :]
    vag_out_of = [cag_out[rho, 2048:4096, :].rearrange("(t two) c -> t (two c)", two=2)
                  for rho in range(4)]
    x1d = nc.dram_tensor("x1d", [DC, 128, NT], F32).ap()

    ctr = {"X": 0, "SC": 0, "W": 0, "ST": 0, "CC": 0, "TE": 0, "ACT": 0, "DVE": 0}
    bank_free = [None] * 8

    with (
        nc.semaphore("sX") as sX,
        nc.semaphore("sSC") as sSC,
        nc.semaphore("sW") as sW,
        nc.semaphore("sST") as sST,
        nc.semaphore("sCC") as sCC,
        nc.semaphore("sTE") as sTE,
        nc.semaphore("sACT") as sACT,
        nc.semaphore("sDVE") as sDVE,
        nc.Block() as block,
    ):
        SEMS = {"X": sX, "SC": sSC, "W": sW, "ST": sST, "CC": sCC,
                "TE": sTE, "ACT": sACT, "DVE": sDVE}

        def mark(name, n=1):
            ctr[name] += n
            return ctr[name]

        def w(eng, name, val):
            if val > 0:
                eng.wait_ge(SEMS[name], val)

        def te_bank_wait(te, i):
            if bank_free[i] is not None:
                nm, v = bank_free[i]
                w(te, nm, v)
                bank_free[i] = None

        guards = []

        def alloc(name, shape, dtype, side=None):
            g = nc.sbuf_tensor(name, shape, dtype, side=side)
            t = g.__enter__()
            return g, t

        ones_g, ones_bf = alloc("ones_bf", [128, 128], BF16, side="left")
        onesf_g, ones_f = alloc("ones_f", [128, 128], F32, side="left")
        stats_g, stats = alloc("stats", [128, 1024], F32, side="left")
        negm_g, negm_b = alloc("negm_b", [128, 1024], F32, side="left")
        rstd_g, rstd_b = alloc("rstd_b", [128, 1024], F32, side="left")
        qbg, qb_sb = alloc("qb_sb", [128, 16], F32, side="left")
        kbg, kb_sb = alloc("kb_sb", [128, 16], F32, side="left")
        obg, ob_sb = alloc("ob_sb", [128, 16], F32, side="left")
        vbg, vb_sb = alloc("vb_sb", [1, 2048], BF16, side="left")
        mkg, mask_sb = alloc("mask_sb", [128, 4, 128], BF16, side="left")
        fcg, fc = alloc("fconsts", [128, 8], F32, side="left")
        guards += [ones_g, onesf_g, stats_g, negm_g, rstd_g, qbg, kbg, obg,
                   vbg, mkg, fcg]
        psg = [nc.psum_tensor(f"pb{i}", [128, 512], F32) for i in range(8)]
        pb = [g.__enter__() for g in psg]

        def p0_sync(sync):
            sync.dma_start(out=qb_sb[:, :], in_=qb).then_inc(sX, 16)
            sync.dma_start(out=kb_sb[:, :], in_=kb).then_inc(sX, 16)
            sync.dma_start(out=ob_sb[:, :], in_=ob).then_inc(sX, 16)
            sync.dma_start(out=mask_sb[:, :, :], in_=masks).then_inc(sX, 16)
        block.sync(p0_sync)
        mark("X", 4)
        v_p0 = ctr["X"] * 16

        def p0_gp(g):
            g.dma_start(out=vb_sb[:, :], in_=vb).then_inc(sW, 16)
        block.gpsimd(p0_gp)
        mark("W")
        v_vb = ctr["W"] * 16

        def p0_vec(v):
            v.memset(ones_bf[:, :], 1.0)
            v.memset(ones_f[:, :], 1.0)
            v.memset(fc[:, 0:1], -CEXP)
            v.memset(fc[:, 1:2], SCALE)
            v.memset(fc[:, 2:3], -0.5).then_inc(sDVE, 1)
        block.vector(p0_vec)
        v_consts = mark("DVE")

        def ln_stats(src, n_tok, xbF, sqF, ready_wait=None):
            nh = n_tok // 512

            def vec(v):
                if ready_wait is not None:
                    nm, val = ready_wait
                    w(v, nm, val)
                ins = None
                for dc in range(DC):
                    v.tensor_copy(xbF[:, dc, 0:n_tok], src(dc))
                    ins = v.tensor_tensor(sqF[:, dc, 0:n_tok], xbF[:, dc, 0:n_tok],
                                          xbF[:, dc, 0:n_tok], ALU.mult)
                ins.then_inc(sDVE, 1)
            block.vector(vec)
            v_prep = mark("DVE")

            def te(t):
                w(t, "DVE", v_prep)
                for i in range(2 * nh):
                    te_bank_wait(t, 4 + i)
                ins = None
                for gi in range(2 * nh):
                    buf = xbF if gi < nh else sqF
                    hh = gi % nh
                    for dc in range(DC):
                        ins = t.matmul(pb[4 + gi][:, 0:512], ones_bf[:, :],
                                       buf[:, dc, 512 * hh:512 * hh + 512],
                                       start=(dc == 0), stop=(dc == DC - 1))
                ins.then_inc(sTE, 1)
            block.tensor(te)
            v_stats = mark("TE")

            def fin_vec(v):
                w(v, "TE", v_stats)
                ins = None
                for h in range(nh):
                    sl = slice(512 * h, 512 * h + 512)
                    v.tensor_scalar(negm_b[:, sl], pb[4 + h][:, 0:512], -RDINV, None, ALU.mult)
                    v.tensor_scalar(stats[:, sl], pb[4 + nh + h][:, 0:512], RDINV, EPS,
                                    ALU.mult, ALU.add)
                    v.tensor_tensor(rstd_b[:, sl], negm_b[:, sl], negm_b[:, sl], ALU.mult)
                    ins = v.tensor_tensor(stats[:, sl], stats[:, sl], rstd_b[:, sl], ALU.subtract)
                ins.then_inc(sDVE, 1)
            block.vector(fin_vec)
            v_fin = mark("DVE")
            for i in range(2 * nh):
                bank_free[4 + i] = ("DVE", v_fin)

            def fin_act(s):
                w(s, "DVE", v_fin)
                ins = None
                for h in range(nh):
                    sl = slice(512 * h, 512 * h + 512)
                    s.activation(stats[:, sl], stats[:, sl], AF.Ln)
                    ins = s.activation(rstd_b[:, sl], stats[:, sl], AF.Exp, scale=fc[:, 2:3])
                ins.then_inc(sACT, 1)
            block.scalar(fin_act)
            return mark("ACT")

        def ln_apply(src, dst, n_tok, tmp, v_rstd):
            dve_m = [0] * DC

            def vec(v):
                w(v, "ACT", v_rstd)
                for dc in range(DC):
                    v.tensor_tensor(tmp[:, 0:n_tok], src(dc), negm_b[:, 0:n_tok], ALU.add)
                    v.tensor_tensor(dst(dc), tmp[:, 0:n_tok], rstd_b[:, 0:n_tok],
                                    ALU.mult).then_inc(sDVE, 1)
                    dve_m[dc] = mark("DVE")
            block.vector(vec)
            return dve_m

        xn1_g, xn1 = alloc("xn1", [128, DC, NT], BF16, side="left")
        x_g, x_sb = alloc("x_sb", [128, DC, NT], F32, side="right")
        xbF_g, xbF = alloc("xbF", [128, DC, NT], BF16, side="right")
        sqF_g, sqF = alloc("sqF", [128, DC, NT], BF16, side="right")
        tmp_g, tmp32 = alloc("tmp32", [128, NT], F32, side="right")

        def p1_sync(sync):
            sync.dma_start(out=x_sb[:, :, :],
                           in_=xT.rearrange("(dc p) t -> p dc t", p=128)).then_inc(sX, 16)
        block.sync(p1_sync)
        mark("X")
        v_xload = ctr["X"] * 16

        v_rstd1 = ln_stats(lambda dc: x_sb[:, dc, :], NT, xbF, sqF,
                           ready_wait=("X", v_xload))
        ln1_m = ln_apply(lambda dc: x_sb[:, dc, :], lambda dc: xn1[:, dc, :], NT,
                         tmp32, v_rstd1)
        v_xn1 = ln1_m[-1]
        tmp_g.__exit__(None, None, None)
        sqF_g.__exit__(None, None, None)
        xbF_g.__exit__(None, None, None)
        x_g.__exit__(None, None, None)

        q_g, q_sb = alloc("q_sb", [128, DC, NT], BF16, side="right")
        wqF_g, wqF = alloc("wqF", [128, DC, 2048], BF16, side="right")
        wvh_g, wvh = alloc("wvh", [128, DC, 1024], BF16, side="right")
        vt_g = [nc.sbuf_tensor(f"vt{i}", [128, 512], BF16, side="right") for i in range(2)]
        vt = [g.__enter__() for g in vt_g]

        n_vtile = 0
        sc_base_v = ctr["SC"]
        v_half_te = 0
        for half in range(2):
            def p2a_gp(g, half=half, v_half_te=v_half_te):
                if half == 0:
                    w(g, "DVE", v_xn1)
                if half == 1:
                    w(g, "TE", v_half_te)
                for dc in range(DC):
                    g.dma_start(out=wvh[:, dc, :],
                                in_=wv[dc, :, 1024 * half:1024 * half + 1024]).then_inc(sW, 16)
                    mark("W")
            block.gpsimd(p2a_gp)
            v_wvh = ctr["W"] * 16

            for tc in range(8):
                for og in range(2):
                    o0 = 1024 * half + 512 * og
                    bk = (tc * 2 + og) % 4
                    v_mm = [0]

                    def p2a_te(t, half=half, tc=tc, og=og, bk=bk, v_wvh=v_wvh, v_mm=v_mm):
                        if tc == 0 and og == 0:
                            w(t, "W", v_wvh)
                            w(t, "W", v_vb)
                            w(t, "DVE", v_xn1)
                        te_bank_wait(t, bk)
                        for dc in range(DC):
                            t.matmul(pb[bk][:, 0:512], xn1[:, dc, 128 * tc:128 * tc + 128],
                                     wvh[:, dc, 512 * og:512 * og + 512],
                                     start=(dc == 0), stop=False)
                        t.matmul(pb[bk][:, 0:512], ones_bf[0:1, :],
                                 vb_sb[0:1, o0:o0 + 512], start=False,
                                 stop=True).then_inc(sTE, 1)
                        v_mm[0] = mark("TE")
                    block.tensor(p2a_te)

                    def p2a_sc(s, tc=tc, og=og, o0=o0, bk=bk, v_mm=v_mm, n=n_vtile):
                        w(s, "TE", v_mm[0])
                        if n >= 2:
                            w(s, "SC", (sc_base_v + n - 1) * 16)
                        bf = n % 2
                        s.activation(vt[bf][:, :], pb[bk][:, 0:512],
                                     AF.Identity).then_inc(sACT, 1)
                        w(s, "ACT", mark("ACT"))
                        s.dma_start(out=vag_in[128 * tc:128 * tc + 128, o0:o0 + 512],
                                    in_=vt[bf][:, :]).then_inc(sSC, 16)
                    block.scalar(p2a_sc)
                    bank_free[bk] = ("ACT", ctr["ACT"])
                    mark("SC")
                    n_vtile += 1
            v_half_te = ctr["TE"]
        v_vag_stores = ctr["SC"] * 16
        for g in reversed(vt_g):
            g.__exit__(None, None, None)
        wvh_g.__exit__(None, None, None)

        def p2b_wqpre(g):
            w(g, "DVE", v_xn1)
            for oc in range(16):
                g.dma_start(out=wqF[:, oc, :], in_=wq[oc, :, :]).then_inc(sW, 16)
                mark("W")
        block.gpsimd(p2b_wqpre)
        v_wqpre = ctr["W"] * 16

        wb_g = [nc.sbuf_tensor(f"wb{i}", [128, 2048], BF16, side="right") for i in range(2)]
        wb = [g.__enter__() for g in wb_g]
        wf_g = [nc.sbuf_tensor(f"wf{i}", [128, 2048], F32, side="right") for i in range(2)]
        wf32 = [g.__enter__() for g in wf_g]
        kt_g = [nc.sbuf_tensor(f"kt{i}", [128, 512], BF16, side="right") for i in range(2)]
        kt = [g.__enter__() for g in kt_g]

        def proj_fm(wsrc, bias_sb, is_k, first_gp_waits=(), resident=None):
            te_oc = [0] * 16
            base_x = ctr["X"]
            n_tile = 0
            sc_base = ctr["SC"]
            cast_m = [0] * 16
            for oc in range(16):
                if resident is None:
                    def gp(sync, oc=oc):
                        if oc == 0:
                            for nm, val in first_gp_waits:
                                w(sync, nm, val)
                        if oc >= 2:
                            w(sync, "ACT", cast_m[oc - 2])
                        sync.dma_start(out=wf32[oc % 2][:, :], in_=wsrc[oc, :, :]).then_inc(sX, 16)
                        mark("X")
                    block.sync(gp)

                    def sc_cast(s, oc=oc):
                        w(s, "X", (base_x + oc + 1) * 16)
                        if oc >= 2:
                            w(s, "TE", te_oc[oc - 2])
                        s.activation(wb[oc % 2][:, :], wf32[oc % 2][:, :],
                                     AF.Identity).then_inc(sACT, 1)
                        cast_m[oc] = mark("ACT")
                    block.scalar(sc_cast)

                for tg in range(2):
                    bk = 4 + (oc * 2 + tg) % 4
                    v_mm = [0]

                    def te(t, oc=oc, tg=tg, bk=bk, v_mm=v_mm):
                        if oc == 0 and tg == 0:
                            w(t, "DVE", v_xn1)
                            if resident is not None:
                                w(t, "W", resident[1])
                        if tg == 0 and resident is None:
                            w(t, "ACT", cast_m[oc])
                        te_bank_wait(t, bk)
                        wsb = wb[oc % 2] if resident is None else resident[0][:, oc, :]
                        ins = None
                        for dc in range(DC):
                            ins = t.matmul(pb[bk][:, 0:512],
                                           wsb[:, 128 * dc:128 * dc + 128],
                                           xn1[:, dc, 512 * tg:512 * tg + 512],
                                           start=(dc == 0), stop=(dc == DC - 1))
                        ins.then_inc(sTE, 1)
                        v_mm[0] = mark("TE")
                        if tg == 1:
                            te_oc[oc] = ctr["TE"]
                    block.tensor(te)

                    def sc(s, oc=oc, tg=tg, bk=bk, v_mm=v_mm, n=n_tile):
                        if oc == 0 and tg == 0:
                            w(s, "X", v_p0)
                        w(s, "TE", v_mm[0])
                        if is_k:
                            if n >= 2:
                                w(s, "SC", (sc_base + n - 1) * 16)
                            bf = n % 2
                            s.activation(kt[bf][:, :], pb[bk][:, 0:512], AF.Identity,
                                         bias=bias_sb[:, oc:oc + 1]).then_inc(sACT, 1)
                            w(s, "ACT", mark("ACT"))
                            s.dma_start(out=kag_in[128 * oc:128 * oc + 128,
                                                   512 * tg:512 * tg + 512],
                                        in_=kt[bf][:, :]).then_inc(sSC, 16)
                        else:
                            s.activation(q_sb[:, oc, 512 * tg:512 * tg + 512],
                                         pb[bk][:, 0:512], AF.Identity,
                                         bias=bias_sb[:, oc:oc + 1]).then_inc(sACT, 1)
                            mark("ACT")
                    block.scalar(sc)
                    bank_free[bk] = ("ACT", ctr["ACT"])
                    if is_k:
                        mark("SC")
                    n_tile += 1
            return cast_m

        k_cast = proj_fm(wk, kb_sb, True,
                         first_gp_waits=(("SC", v_vag_stores), ("TE", v_half_te)))
        v_kag_stores = ctr["SC"] * 16

        def p2b_cc(g):
            w(g, "SC", v_kag_stores)
            g.collective_compute(
                "AllGather", ALU.bypass,
                replica_groups=[[0, 1, 2, 3], [4, 5, 6, 7]],
                ins=[cag_in.opt()], outs=[cag_out.opt()],
            ).then_inc(sCC, 1)
        block.gpsimd(p2b_cc)
        mark("CC", 1)

        proj_fm(wq, qb_sb, False, resident=(wqF, v_wqpre))
        v_q_done = ctr["ACT"]

        if DEBUG:
            def dbg_kin(sync):
                w(sync, "SC", v_kag_stores)
                sync.dma_start(out=dbg["kin"], in_=kag_in).then_inc(sST, 16)
            block.sync(dbg_kin)
            mark("ST")

            def dbg_p3(sync):
                w(sync, "DVE", v_xn1)
                sync.dma_start(out=dbg["xn1"], in_=xn1[:, :, :]).then_inc(sST, 16)
                w(sync, "ACT", v_q_done)
                sync.dma_start(out=dbg["q"], in_=q_sb[:, :, :]).then_inc(sST, 16)
                w(sync, "CC", 1)
                sync.dma_start(out=dbg["kag"], in_=kag_out).then_inc(sST, 16)
                for rho in range(4):
                    sync.dma_start(out=dbg["vag"][rho], in_=vag_out_of[rho]).then_inc(sST, 16)
            block.sync(dbg_p3)
            mark("ST", 7)

        for g in reversed(kt_g):
            g.__exit__(None, None, None)
        for g in reversed(wf_g):
            g.__exit__(None, None, None)
        for g in reversed(wb_g):
            g.__exit__(None, None, None)
        wqF_g.__exit__(None, None, None)
        xn1_g.__exit__(None, None, None)

        av_g, av_all = alloc("av_all", [128, DC, NT], BF16, side="left")
        k2_g = [nc.sbuf_tensor(f"k2_{i}", [128, 2, 4, 1024], BF16, side="left") for i in range(2)]
        k2 = [g.__enter__() for g in k2_g]
        v2_g = [nc.sbuf_tensor(f"v2_{i}", [128, 4, 8, 256], BF16, side="left") for i in range(2)]
        v2 = [g.__enter__() for g in v2_g]
        rb_g, rb_sb = alloc("rb_sb", [128, 1024], F32, side="left")

        aoff = {}
        acur = 0
        for j in range(8):
            for kc in range(4):
                aoff[(j, kc)] = acur
                acur += 1024 - 128 * j
        ATOT = acur
        pta_g, pta = alloc("pta", [128, ATOT], BF16, side="left")

        parts = []
        for j in range(8):
            wj = 1024 - 128 * j
            for kc in range(4):
                parts.append((j, kc, 0, min(512, wj)))
                if wj > 512:
                    parts.append((j, kc, 512, wj - 512))
        npp = len(parts)

        act_base4 = ctr["ACT"]
        dve_base4 = ctr["DVE"]

        def exp_mark(h, n):
            return act_base4 + h * npp + n + 1

        def msk_mark(h, k):
            return dve_base4 + h * 33 + k + 1

        def evac_mark(h):
            return dve_base4 + h * 33 + 33

        kv_m = [0] * 8
        hp_te_done = [0] * 8
        groups_mark = {-1: 0}

        for hp in range(8):
            def p4_sync_hp(sync, hp=hp):
                if hp == 0:
                    w(sync, "CC", 1)
                    w(sync, "ACT", v_q_done)
                if hp >= 2:
                    w(sync, "TE", hp_te_done[hp - 2])
                b = hp % 2
                for rho in range(4):
                    sync.dma_start(
                        out=k2[b][:, :, rho, :],
                        in_=kag_out[rho, 256 * hp:256 * hp + 256, :]
                        .rearrange("(e p) c -> p e c", p=128)).then_inc(sX, 16)
                    mark("X")
                    sync.dma_start(
                        out=v2[b][:, rho, :, :],
                        in_=vag_out_of[rho][:, 256 * hp:256 * hp + 256]
                        .rearrange("(g p) c -> p g c", p=128)).then_inc(sX, 16)
                    mark("X")
                kv_m[hp] = ctr["X"] * 16
            block.sync(p4_sync_hp)

            for e in range(2):
                h = 2 * hp + e
                b = hp % 2
                s_m = [0] * npp

                def p4_te_scores(t, hp=hp, e=e, h=h, b=b, s_m=s_m):
                    if h == 0:
                        for i in range(8):
                            te_bank_wait(t, i)
                        w(t, "DVE", v_consts)
                    w(t, "X", kv_m[hp])
                    for n, (j, kc, plo, pn) in enumerate(parts):
                        if n >= 2:
                            w(t, "ACT", exp_mark(h, n - 2))
                        t.matmul(pb[n % 2][:, 0:pn],
                                 k2[b][:, e, kc, 128 * j:128 * j + 128],
                                 q_sb[:, h, 128 * j + plo:128 * j + plo + pn],
                                 start=True, stop=True).then_inc(sTE, 1)
                        s_m[n] = mark("TE")
                block.tensor(p4_te_scores)

                def p4_act(s, h=h, s_m=s_m):
                    if h == 0:
                        w(s, "DVE", v_consts)
                    w(s, "TE", groups_mark[h - 1])
                    for n, (j, kc, plo, pn) in enumerate(parts):
                        w(s, "TE", s_m[n])
                        s.activation(pta[:, aoff[(j, kc)] + plo:aoff[(j, kc)] + plo + pn],
                                     pb[n % 2][:, 0:pn],
                                     AF.Exp, bias=fc[:, 0:1], scale=fc[:, 1:2]).then_inc(sACT, 1)
                        assert mark("ACT") == exp_mark(h, n)
                block.scalar(p4_act)
                v_exp_last = exp_mark(h, npp - 1)
                v_exp_A = exp_mark(h, max(n for n, p_ in enumerate(parts)
                                          if p_[0] <= 3 and p_[2] == 0))
                v_msk_A = msk_mark(h, 15)

                def p4_dve(v, h=h):
                    if h == 0:
                        w(v, "X", v_p0)
                    w(v, "TE", groups_mark[h - 1])
                    kk = 0
                    for j in range(8):
                        for kc in range(4):
                            n = parts.index((j, kc, 0, min(512, 1024 - 128 * j)))
                            w(v, "ACT", exp_mark(h, n))
                            o0 = aoff[(j, kc)]
                            v.tensor_tensor(pta[:, o0:o0 + 128], pta[:, o0:o0 + 128],
                                            mask_sb[:, kc, :], ALU.mult).then_inc(sDVE, 1)
                            assert mark("DVE") == msk_mark(h, kk)
                            kk += 1
                block.vector(p4_dve)
                v_msk_last = msk_mark(h, 31)

                def p4_te_groups(t, h=h, b=b, e=e):
                    if h > 0:
                        w(t, "DVE", evac_mark(h - 1))
                    w(t, "ACT", v_exp_A)
                    w(t, "DVE", v_msk_A)
                    for j in range(4):
                        for kc in range(4):
                            o0 = aoff[(j, kc)]
                            nA = 512 - 128 * j
                            t.matmul(pb[2][:, 128 * j:512],
                                     v2[b][:, kc, j, 128 * e:128 * e + 128],
                                     pta[:, o0:o0 + nA],
                                     start=(j == 0 and kc == 0),
                                     stop=(j == 3 and kc == 3))
                    for j in range(4):
                        for kc in range(4):
                            o0 = aoff[(j, kc)]
                            nA = 512 - 128 * j
                            t.matmul(pb[4][:, 128 * j:512], ones_bf[:, :],
                                     pta[:, o0:o0 + nA],
                                     start=(j == 0 and kc == 0),
                                     stop=(j == 3 and kc == 3))
                    w(t, "ACT", v_exp_last)
                    w(t, "DVE", v_msk_last)
                    for j in range(8):
                        for kc in range(4):
                            o0 = aoff[(j, kc)]
                            if j <= 3:
                                lo, q0 = 512 - 128 * j, 0
                            else:
                                lo, q0 = 0, 128 * j - 512
                            nB = 512 - q0
                            t.matmul(pb[3][:, q0:512],
                                     v2[b][:, kc, j, 128 * e:128 * e + 128],
                                     pta[:, o0 + lo:o0 + lo + nB],
                                     start=(j == 0 and kc == 0),
                                     stop=(j == 7 and kc == 3))
                    ins = None
                    for j in range(8):
                        for kc in range(4):
                            o0 = aoff[(j, kc)]
                            if j <= 3:
                                lo, q0 = 512 - 128 * j, 0
                            else:
                                lo, q0 = 0, 128 * j - 512
                            nB = 512 - q0
                            ins = t.matmul(pb[5][:, q0:512], ones_bf[:, :],
                                           pta[:, o0 + lo:o0 + lo + nB],
                                           start=(j == 0 and kc == 0),
                                           stop=(j == 7 and kc == 3))
                    ins.then_inc(sTE, 1)
                block.tensor(p4_te_groups)
                v_av_te = mark("TE")
                groups_mark[h] = v_av_te
                hp_te_done[hp] = v_av_te

                def p4_evac(v, h=h, v_av_te=v_av_te):
                    w(v, "TE", v_av_te)
                    v.reciprocal(rb_sb[:, 0:512], pb[4][:, 0:512])
                    v.reciprocal(rb_sb[:, 512:1024], pb[5][:, 0:512])
                    v.tensor_tensor(av_all[:, h, 0:512], pb[2][:, 0:512],
                                    rb_sb[:, 0:512], ALU.mult)
                    v.tensor_tensor(av_all[:, h, 512:1024], pb[3][:, 0:512],
                                    rb_sb[:, 512:1024], ALU.mult).then_inc(sDVE, 1)
                    assert mark("DVE") == evac_mark(h)
                block.vector(p4_evac)
        for i in (2, 3, 4, 5):
            bank_free[i] = ("DVE", evac_mark(15))
        bank_free[0] = ("ACT", exp_mark(15, npp - 2))
        bank_free[1] = ("ACT", exp_mark(15, npp - 1))
        v_av_done = ctr["DVE"]

        if DEBUG:
            def dbg_p4(sync):
                w(sync, "DVE", v_av_done)
                sync.dma_start(out=dbg["av"], in_=av_all[:, :, :]).then_inc(sST, 16)
            block.sync(dbg_p4)
            mark("ST")

        pta_g.__exit__(None, None, None)
        rb_g.__exit__(None, None, None)
        for g in reversed(v2_g):
            g.__exit__(None, None, None)
        for g in reversed(k2_g):
            g.__exit__(None, None, None)
        q_g.__exit__(None, None, None)

        mix_g, attnmix = alloc("attnmix", [128, DC, NT], F32, side="right")
        wb2_g = [nc.sbuf_tensor(f"wo{i}", [128, 2048], BF16, side="right") for i in range(2)]
        wb2 = [g.__enter__() for g in wb2_g]
        wf5_g = [nc.sbuf_tensor(f"wf5{i}", [128, 2048], F32, side="right") for i in range(2)]
        wf5 = [g.__enter__() for g in wf5_g]

        te_oc5 = [0] * 16
        base_x5 = ctr["X"]
        cast5_m = [0] * 16
        for oc in range(16):
            def p5_gp(sync, oc=oc):
                if oc == 0:
                    w(sync, "DVE", v_av_done)
                if oc >= 2:
                    w(sync, "ACT", cast5_m[oc - 2])
                sync.dma_start(out=wf5[oc % 2][:, :], in_=wo[oc, :, :]).then_inc(sX, 16)
                mark("X")
            block.sync(p5_gp)

            def p5_cast(s, oc=oc):
                w(s, "X", (base_x5 + oc + 1) * 16)
                if oc >= 2:
                    w(s, "TE", te_oc5[oc - 2])
                s.activation(wb2[oc % 2][:, :], wf5[oc % 2][:, :],
                             AF.Identity).then_inc(sACT, 1)
                cast5_m[oc] = mark("ACT")
            block.scalar(p5_cast)

            for tg in range(2):
                bk = (oc * 2 + tg) % 4
                v_mm = [0]

                def p5_te(t, oc=oc, tg=tg, bk=bk, v_mm=v_mm):
                    if oc == 0 and tg == 0:
                        w(t, "DVE", v_av_done)
                    if tg == 0:
                        w(t, "ACT", cast5_m[oc])
                    te_bank_wait(t, bk)
                    ins = None
                    for dc in range(DC):
                        ins = t.matmul(pb[bk][:, 0:512],
                                       wb2[oc % 2][:, 128 * dc:128 * dc + 128],
                                       av_all[:, dc, 512 * tg:512 * tg + 512],
                                       start=(dc == 0), stop=(dc == DC - 1))
                    ins.then_inc(sTE, 1)
                    v_mm[0] = mark("TE")
                    if tg == 1:
                        te_oc5[oc] = ctr["TE"]
                block.tensor(p5_te)

                def p5_sc(s, oc=oc, tg=tg, bk=bk, v_mm=v_mm):
                    w(s, "TE", v_mm[0])
                    s.activation(attnmix[:, oc, 512 * tg:512 * tg + 512], pb[bk][:, 0:512],
                                 AF.Identity, bias=ob_sb[:, oc:oc + 1]).then_inc(sACT, 1)
                block.scalar(p5_sc)
                bank_free[bk] = ("ACT", mark("ACT"))
        v_mix = ctr["ACT"]
        v_te_p5 = ctr["TE"]

        if DEBUG:
            def dbg_p5(sync):
                w(sync, "ACT", v_mix)
                sync.dma_start(out=dbg["mix"], in_=attnmix[:, :, :]).then_inc(sST, 16)
            block.sync(dbg_p5)
            mark("ST")

        for g in reversed(wf5_g):
            g.__exit__(None, None, None)
        for g in reversed(wb2_g):
            g.__exit__(None, None, None)
        av_g.__exit__(None, None, None)

        xn2_g, xn2 = alloc("xn2", [128, DC, NT], BF16, side="left")
        xbF2_g, xbF2 = alloc("xbF2", [128, DC, NT], BF16, side="right")
        sqF2_g, sqF2 = alloc("sqF2", [128, DC, NT], BF16, side="right")
        xs_g = [nc.sbuf_tensor(f"xs{i}", [128, NT], F32, side="right") for i in range(4)]
        xs = [g.__enter__() for g in xs_g]
        tmp2_g, tmp2 = alloc("tmp2", [128, NT], F32, side="right")

        v_rstd2 = ln_stats(lambda dc: attnmix[:, dc, :], NT, xbF2, sqF2,
                           ready_wait=("ACT", v_mix))

        x_dve_m = [0] * DC
        xs_loaded = [0] * DC
        spill_m = [0] * DC
        for dc in range(DC):
            def p6_load(sync, dc=dc):
                if dc == 0:
                    w(sync, "TE", v_te_p5)
                if dc >= 4:
                    w(sync, "DVE", x_dve_m[dc - 4])
                sync.dma_start(out=xs[dc % 4][:, :],
                               in_=xT.rearrange("(dc p) t -> p dc t", p=128)[:, dc, :]
                               ).then_inc(sX, 16)
                mark("X")
                xs_loaded[dc] = ctr["X"] * 16
            block.sync(p6_load)

            def p6_app(v, dc=dc):
                if dc == 0:
                    w(v, "ACT", v_rstd2)
                w(v, "X", xs_loaded[dc])
                v.tensor_tensor(tmp2[:, :], attnmix[:, dc, :], negm_b[:, :], ALU.add)
                v.tensor_tensor(tmp2[:, :], tmp2[:, :], rstd_b[:, :], ALU.mult)
                v.tensor_tensor(attnmix[:, dc, :], tmp2[:, :], xs[dc % 4][:, :],
                                ALU.add).then_inc(sDVE, 1)
                x_dve_m[dc] = mark("DVE")
            block.vector(p6_app)

            def p6_spill(sync, dc=dc):
                w(sync, "DVE", x_dve_m[dc])
                sync.dma_start(out=x1d[dc, :, :], in_=attnmix[:, dc, :]).then_inc(sX, 16)
                mark("X")
                spill_m[dc] = ctr["X"] * 16
            block.sync(p6_spill)

        v_rstd3 = ln_stats(lambda dc: attnmix[:, dc, :], NT, xbF2, sqF2,
                           ready_wait=("DVE", x_dve_m[-1]))
        ln3_m = ln_apply(lambda dc: attnmix[:, dc, :], lambda dc: xn2[:, dc, :], NT,
                         tmp2, v_rstd3)
        v_xn2 = ln3_m[-1]

        if DEBUG:
            def dbg_p6(sync):
                w(sync, "DVE", v_xn2)
                sync.dma_start(out=dbg["xn2"], in_=xn2[:, :, :]).then_inc(sST, 16)
            block.sync(dbg_p6)
            mark("ST")

        tmp2_g.__exit__(None, None, None)
        for g in reversed(xs_g):
            g.__exit__(None, None, None)
        sqF2_g.__exit__(None, None, None)
        xbF2_g.__exit__(None, None, None)
        mix_g.__exit__(None, None, None)

        h_g, h_sb = alloc("h_sb", [128, ICN, NT], BF16, side="right")
        gu_g = [nc.sbuf_tensor(f"gu{i}", [128, 2048], BF16, side="right") for i in range(4)]
        gub = [g.__enter__() for g in gu_g]
        gt_g = [nc.sbuf_tensor(f"gt{i}", [128, 512], BF16, side="right") for i in range(4)]
        gt = [g.__enter__() for g in gt_g]

        w7 = [0] * ICN
        te7 = {}
        dve7 = {}
        act7 = {}
        for ic in range(ICN):
            def p7_gp(g, ic=ic):
                if ic == 0:
                    w(g, "DVE", v_xn2)
                if ic >= 2:
                    w(g, "DVE", dve7[(ic - 2, 1)])
                g.dma_start(out=gub[ic % 2][:, :], in_=wgu[ic, :, :]).then_inc(sW, 16)
                mark("W")
                g.dma_start(out=gub[2 + ic % 2][:, :], in_=wgu[64 + ic, :, :]).then_inc(sW, 16)
                mark("W")
                w7[ic] = ctr["W"] * 16
            block.gpsimd(p7_gp)

            def p7_te(t, ic=ic):
                if ic == 0:
                    w(t, "DVE", v_xn2)
                    for i in range(8):
                        te_bank_wait(t, i)
                w(t, "W", w7[ic])
                if ic >= 2:
                    w(t, "ACT", act7[(ic - 2, 1)])
                    w(t, "DVE", dve7[(ic - 2, 1)])
                for tg in range(2):
                    bg = (ic % 2) * 4 + tg * 2
                    bu = bg + 1
                    for dc in range(DC):
                        t.matmul(pb[bg][:, 0:512], gub[ic % 2][:, 128 * dc:128 * dc + 128],
                                 xn2[:, dc, 512 * tg:512 * tg + 512],
                                 start=(dc == 0), stop=(dc == DC - 1))
                    ins = None
                    for dc in range(DC):
                        ins = t.matmul(pb[bu][:, 0:512],
                                       gub[2 + ic % 2][:, 128 * dc:128 * dc + 128],
                                       xn2[:, dc, 512 * tg:512 * tg + 512],
                                       start=(dc == 0), stop=(dc == DC - 1))
                    ins.then_inc(sTE, 1)
                    te7[(ic, tg)] = mark("TE")
            block.tensor(p7_te)

            def p7_sc(s, ic=ic):
                for tg in range(2):
                    bg = (ic % 2) * 4 + tg * 2
                    w(s, "TE", te7[(ic, tg)])
                    if ic >= 2:
                        w(s, "DVE", dve7[(ic - 2, tg)])
                    s.activation(gt[(ic % 2) * 2 + tg][:, :], pb[bg][:, 0:512],
                                 AF.Silu).then_inc(sACT, 1)
                    act7[(ic, tg)] = mark("ACT")
            block.scalar(p7_sc)

            def p7_dve(v, ic=ic):
                if ic == 0:
                    w(v, "X", spill_m[-1])
                for tg in range(2):
                    bu = (ic % 2) * 4 + tg * 2 + 1
                    w(v, "ACT", act7[(ic, tg)])
                    v.tensor_tensor(h_sb[:, ic, 512 * tg:512 * tg + 512],
                                    gt[(ic % 2) * 2 + tg][:, :], pb[bu][:, 0:512],
                                    ALU.mult).then_inc(sDVE, 1)
                    dve7[(ic, tg)] = mark("DVE")
            block.vector(p7_dve)
        for i in range(8):
            ic_last = ICN - 2 + (i // 4)
            tg_last = (i % 4) // 2
            if i % 2 == 0:
                bank_free[i] = ("ACT", act7[(ic_last, tg_last)])
            else:
                bank_free[i] = ("DVE", dve7[(ic_last, tg_last)])

        for g in reversed(gt_g):
            g.__exit__(None, None, None)
        for g in reversed(gu_g):
            g.__exit__(None, None, None)
        xn2_g.__exit__(None, None, None)

        wd_g = [nc.sbuf_tensor(f"wdb{i}", [128, 4096], BF16, side="right") for i in range(2)]
        wdb = [g.__enter__() for g in wd_g]
        fo_g, fo = alloc("ffnout", [128, DC, 512], F32, side="right")
        sq8_g, sq8 = alloc("sq8", [128, 8, 512], BF16, side="right")
        x1r_g = [nc.sbuf_tensor(f"x1r{i}", [128, 512], F32, side="right") for i in range(2)]
        x1r = [g.__enter__() for g in x1r_g]
        tmp4 = stats

        v_spill_all = spill_m[-1]
        wd_seq_te = []
        v_out_prev = [0]
        v_dump = [0]

        for tg in range(2):
            w8 = {}
            te8h = {}
            act8 = [0] * 16
            dve8 = [0] * 16

            for oc in range(16):
                def p8_gp(g, tg=tg, oc=oc):
                    if tg == 0 and oc == 0:
                        w(g, "TE", te7[(ICN - 1, 1)])
                    for hf in range(2):
                        nseq = len(wd_seq_te)
                        if nseq >= 2:
                            mrk = wd_seq_te[nseq - 2]
                            if mrk[0]:
                                w(g, "TE", mrk[0])
                        g.dma_start(out=wdb[(oc * 2 + hf) % 2][:, :],
                                    in_=wd[oc, :, 4096 * hf:4096 * hf + 4096]).then_inc(sW, 16)
                        mark("W")
                        w8[(oc, hf)] = ctr["W"] * 16
                        wd_seq_te.append([0])
                block.gpsimd(p8_gp)

                bk0 = (oc % 2) * 2
                bk1 = bk0 + 1

                def p8_te(t, tg=tg, oc=oc, bk0=bk0, bk1=bk1):
                    if tg == 0 and oc == 0:
                        w(t, "DVE", dve7[(ICN - 1, 1)])
                    te_bank_wait(t, bk0)
                    te_bank_wait(t, bk1)
                    for hf in range(2):
                        bk = bk0 if hf == 0 else bk1
                        w(t, "W", w8[(oc, hf)])
                        ins = None
                        for ic2 in range(32):
                            ic = 32 * hf + ic2
                            ins = t.matmul(pb[bk][:, 0:512],
                                           wdb[(oc * 2 + hf) % 2][:, 128 * ic2:128 * ic2 + 128],
                                           h_sb[:, ic, 512 * tg:512 * tg + 512],
                                           start=(ic2 == 0), stop=(ic2 == 31))
                        ins.then_inc(sTE, 1)
                        wd_seq_te[tg * 32 + oc * 2 + hf][0] = mark("TE")
                    te8h[oc] = ctr["TE"]
                block.tensor(p8_te)

                def p8_sc(s, tg=tg, oc=oc, bk0=bk0):
                    if tg == 1 and oc == 0:
                        w(s, "ST", v_out_prev[0])
                    w(s, "TE", te8h[oc])
                    s.activation(fo[:, oc, :], pb[bk0][:, 0:512],
                                 AF.Identity).then_inc(sACT, 1)
                    act8[oc] = mark("ACT")
                block.scalar(p8_sc)
                bank_free[bk0] = ("ACT", act8[oc])

                def p8_dve(v, tg=tg, oc=oc, bk1=bk1):
                    w(v, "ACT", act8[oc])
                    v.tensor_tensor(fo[:, oc, :], fo[:, oc, :], pb[bk1][:, 0:512],
                                    ALU.add).then_inc(sDVE, 1)
                    dve8[oc] = mark("DVE")
                block.vector(p8_dve)
                bank_free[bk1] = ("DVE", dve8[oc])

            def p8_sx(t, tg=tg):
                w(t, "DVE", dve8[15])
                te_bank_wait(t, 4)
                te_bank_wait(t, 5)
                te_bank_wait(t, 6)
                ins = None
                for oc in range(16):
                    ins = t.matmul(pb[4][:, 0:512], ones_f[:, :], fo[:, oc, :],
                                   start=(oc == 0), stop=(oc == 15))
                ins.then_inc(sTE, 1)
            block.tensor(p8_sx)
            v_sx = mark("TE")

            sq_m = [0, 0]
            sxx_m = [0, 0]
            for grp in range(2):
                def p8_sq(v, tg=tg, grp=grp):
                    w(v, "DVE", dve8[15])
                    if grp == 1:
                        w(v, "TE", sxx_m[0])
                    ins = None
                    for i in range(8):
                        oc = 8 * grp + i
                        ins = v.tensor_tensor(sq8[:, i, :], fo[:, oc, :], fo[:, oc, :],
                                              ALU.mult)
                    ins.then_inc(sDVE, 1)
                    sq_m[grp] = mark("DVE")
                block.vector(p8_sq)

                def p8_sxx(t, tg=tg, grp=grp):
                    w(t, "DVE", sq_m[grp])
                    ins = None
                    for i in range(8):
                        ins = t.matmul(pb[5 + grp][:, 0:512], ones_bf[:, :], sq8[:, i, :],
                                       start=(i == 0), stop=(i == 7))
                    ins.then_inc(sTE, 1)
                    sxx_m[grp] = mark("TE")
                block.tensor(p8_sxx)

            def p8_fin_act0(s, tg=tg):
                w(s, "TE", sxx_m[1])
                s.activation(stats[:, 512:1024], pb[6][:, 0:512],
                             AF.Identity).then_inc(sACT, 1)
            block.scalar(p8_fin_act0)
            v_cp6 = mark("ACT")

            def p8_fin_vec(v, tg=tg):
                w(v, "TE", v_sx)
                w(v, "ACT", v_cp6)
                v.tensor_scalar(negm_b[:, 0:512], pb[4][:, 0:512], -RDINV, None, ALU.mult)
                v.tensor_tensor(stats[:, 0:512], pb[5][:, 0:512], stats[:, 512:1024], ALU.add)
                v.tensor_scalar(stats[:, 0:512], stats[:, 0:512], RDINV, EPS, ALU.mult, ALU.add)
                v.tensor_tensor(rstd_b[:, 0:512], negm_b[:, 0:512], negm_b[:, 0:512], ALU.mult)
                v.tensor_tensor(stats[:, 0:512], stats[:, 0:512], rstd_b[:, 0:512],
                                ALU.subtract).then_inc(sDVE, 1)
            block.vector(p8_fin_vec)
            v_fin4 = mark("DVE")
            bank_free[4] = ("DVE", v_fin4)
            bank_free[5] = ("DVE", v_fin4)
            bank_free[6] = ("ACT", v_cp6)

            def p8_fin_act(s, tg=tg, v_fin4=v_fin4):
                w(s, "DVE", v_fin4)
                s.activation(stats[:, 0:512], stats[:, 0:512], AF.Ln)
                s.activation(rstd_b[:, 0:512], stats[:, 0:512], AF.Exp,
                             scale=fc[:, 2:3]).then_inc(sACT, 1)
            block.scalar(p8_fin_act)
            v_rstd4 = mark("ACT")

            if DEBUG:
                def dbg_p8(sync, tg=tg):
                    if tg == 0:
                        w(sync, "DVE", dve7[(ICN - 1, 1)])
                        sync.dma_start(out=dbg["h"], in_=h_sb[:, :, :]).then_inc(sST, 16)
                    w(sync, "DVE", dve8[15])
                    sync.dma_start(out=dbg["ffn"][tg], in_=fo[:, :, :]).then_inc(sST, 16)
                block.sync(dbg_p8)
                mark("ST", 2 if tg == 0 else 1)
                v_dump[0] = ctr["ST"] * 16

            x1l = [0] * 16
            app8 = [0] * 16
            for oc in range(16):
                def p8_x1(sync, tg=tg, oc=oc):
                    if tg == 0 and oc == 0:
                        w(sync, "X", v_spill_all)
                        w(sync, "TE", te7[(ICN - 1, 1)])
                    if oc >= 2:
                        w(sync, "DVE", app8[oc - 2])
                    sync.dma_start(out=x1r[oc % 2][:, :],
                                   in_=x1d[oc, :, 512 * tg:512 * tg + 512]).then_inc(sX, 16)
                    mark("X")
                    x1l[oc] = ctr["X"] * 16
                block.sync(p8_x1)

                def p8_app(v, tg=tg, oc=oc, v_rstd4=v_rstd4):
                    if oc == 0:
                        if DEBUG:
                            w(v, "ST", v_dump[0])
                        w(v, "ACT", v_rstd4)
                    w(v, "X", x1l[oc])
                    v.tensor_tensor(tmp4[:, 0:512], fo[:, oc, :], negm_b[:, 0:512], ALU.add)
                    v.tensor_tensor(tmp4[:, 0:512], tmp4[:, 0:512], rstd_b[:, 0:512], ALU.mult)
                    v.tensor_tensor(fo[:, oc, :], tmp4[:, 0:512], x1r[oc % 2][:, :],
                                    ALU.add).then_inc(sDVE, 1)
                    app8[oc] = mark("DVE")
                block.vector(p8_app)

            def p8_out(sync, tg=tg, app8=app8):
                w(sync, "DVE", app8[15])
                sync.dma_start(
                    out=outT.rearrange("(dc p) t -> p dc t", p=128)[:, :, 512 * tg:512 * tg + 512],
                    in_=fo[:, :, :]).then_inc(sST, 16)
            block.sync(p8_out)
            mark("ST")
            v_out_prev[0] = ctr["ST"] * 16

        def p_final(sync):
            w(sync, "ST", ctr["ST"] * 16)
        block.sync(p_final)

        for g in reversed(x1r_g):
            g.__exit__(None, None, None)
        sq8_g.__exit__(None, None, None)
        fo_g.__exit__(None, None, None)
        for g in reversed(wd_g):
            g.__exit__(None, None, None)
        h_g.__exit__(None, None, None)
        for g in reversed(psg):
            g.__exit__(None, None, None)
        for g in reversed(guards):
            g.__exit__(None, None, None)

    return nc



def _weight_images(inputs):
    qw = np.asarray(inputs["L0_attn_q_w"], np.float32)
    kw = np.asarray(inputs["L0_attn_k_w"], np.float32)
    vw = np.asarray(inputs["L0_attn_v_w"], np.float32)
    ow = np.asarray(inputs["L0_attn_o_w"], np.float32)
    gu = np.asarray(inputs["L0_ffn_w_gate_up"], np.float32)
    dw = np.asarray(inputs["L0_ffn_w_down"], np.float32)

    def fm_img(wmat):
        return np.ascontiguousarray(
            wmat.reshape(16, 128, 16, 128).transpose(0, 3, 2, 1).reshape(16, 128, 2048))
    wq_img = fm_img(qw)
    wk_img = fm_img(kw)
    wo_img = fm_img(ow)
    wv_img = np.ascontiguousarray(vw.T.reshape(16, 128, 2048))
    wgu_img = np.ascontiguousarray(
        gu.reshape(128, 128, 16, 128).transpose(0, 3, 2, 1).reshape(128, 128, 2048))
    wd_img = np.ascontiguousarray(
        dw.reshape(16, 128, 64, 128).transpose(0, 3, 2, 1).reshape(16, 128, 8192))
    return wq_img, wk_img, wv_img, wo_img, wgu_img, wd_img


def _core_masks(r):
    ki = np.arange(128)[:, None]
    qi = np.arange(128)[None, :]
    m = np.zeros((4, 128, 128), dtype=np.float32)
    for kc in range(4):
        m[kc] = (128 * kc + ki <= 128 * r + qi).astype(np.float32)
    return np.ascontiguousarray(m.transpose(1, 0, 2)).astype(ml_dtypes.bfloat16)


_nc_cache = {}


def kernel(**inputs):
    key = "prog_dbg" if DEBUG else "prog"
    if key not in _nc_cache:
        _nc_cache[key] = build_program()
    nc = _nc_cache[key]

    x = np.asarray(inputs["L0_attn_hidden_states"], dtype=np.float32)
    wq_img, wk_img, wv_img, wo_img, wgu_img, wd_img = _weight_images(inputs)
    qb_h = np.ascontiguousarray(np.asarray(inputs["L0_attn_q_b"], np.float32).reshape(16, 128).T)
    kb_h = np.ascontiguousarray(np.asarray(inputs["L0_attn_k_b"], np.float32).reshape(16, 128).T)
    vb_h = np.ascontiguousarray(np.asarray(inputs["L0_attn_v_b"], np.float32).reshape(1, 2048))
    ob_h = np.ascontiguousarray(np.asarray(inputs["L0_attn_o_b"], np.float32).reshape(16, 128).T)

    in_maps = []
    idx_of_core = []
    for c in range(8):
        b, r = c // 4, c % 4
        idx = np.concatenate([np.arange(512 * g + 128 * r, 512 * g + 128 * r + 128)
                              for g in range(8)])
        idx_of_core.append((b, idx))
        xT_c = np.ascontiguousarray(x[b].T[:, idx])
        in_maps.append({
            "xT": xT_c,
            "wq": wq_img, "wk": wk_img, "wv": wv_img, "wo": wo_img,
            "wgu": wgu_img, "wd": wd_img,
            "qb": qb_h, "kb": kb_h, "vb": vb_h, "ob": ob_h,
            "masks": _core_masks(r),
        })

    res = run_bass_kernel_spmd(nc, in_maps, core_ids=list(range(8)))

    out = np.empty((NB, S, D), dtype=np.float32)
    for c in range(8):
        b, idx = idx_of_core[c]
        out[b, idx, :] = np.asarray(res.results[c]["outT"]).T
    if DEBUG:
        kernel.debug_results = res.results
    return out


# version 11
# speedup vs baseline: 1.0342x; 1.0342x over previous
import numpy as np
import ml_dtypes
import concourse.bass as bass
import concourse.mybir as mybir
from concourse.bass_utils import run_bass_kernel_spmd

F32 = mybir.dt.float32
BF16 = mybir.dt.bfloat16
AF = mybir.ActivationFunctionType
ALU = mybir.AluOpType

D = 2048
NH = 16
HD = 128
NB = 2
S = 4096
INTER = 8192
NT = 1024
DC = 16
ICN = 64
CEXP = 28.0
SCALE = 1.0 / float(np.sqrt(128.0))
EPS = 1e-5
RDINV = 1.0 / float(D)

DEBUG = False


def build_program():
    nc = bass.Bass("TRN2", target_bir_lowering=False, detect_race_conditions=False)

    xT = nc.dram_tensor("xT", [D, NT], F32, kind="ExternalInput").ap()
    wq = nc.dram_tensor("wq", [16, 128, 2048], F32, kind="ExternalInput").ap()
    wk = nc.dram_tensor("wk", [16, 128, 2048], F32, kind="ExternalInput").ap()
    wv = nc.dram_tensor("wv", [16, 128, 2048], F32, kind="ExternalInput").ap()
    wo = nc.dram_tensor("wo", [16, 128, 2048], F32, kind="ExternalInput").ap()
    wgu = nc.dram_tensor("wgu", [128, 128, 2048], F32, kind="ExternalInput").ap()
    wd = nc.dram_tensor("wd", [16, 128, 8192], F32, kind="ExternalInput").ap()
    qb = nc.dram_tensor("qb", [128, 16], F32, kind="ExternalInput").ap()
    kb = nc.dram_tensor("kb", [128, 16], F32, kind="ExternalInput").ap()
    vb = nc.dram_tensor("vb", [1, 2048], F32, kind="ExternalInput").ap()
    ob = nc.dram_tensor("ob", [128, 16], F32, kind="ExternalInput").ap()
    masks = nc.dram_tensor("masks", [128, 4, 128], BF16, kind="ExternalInput").ap()
    outT = nc.dram_tensor("outT", [D, NT], F32, kind="ExternalOutput").ap()

    dbg = {}
    if DEBUG:
        dbg["xn1"] = nc.dram_tensor("d_xn1", [128, DC, NT], BF16, kind="ExternalOutput").ap()
        dbg["q"] = nc.dram_tensor("d_q", [128, DC, NT], BF16, kind="ExternalOutput").ap()
        dbg["kag"] = nc.dram_tensor("d_kag", [4, 2048, 1024], BF16, kind="ExternalOutput").ap()
        dbg["vag"] = nc.dram_tensor("d_vag", [4, 1024, 2048], BF16, kind="ExternalOutput").ap()
        dbg["av"] = nc.dram_tensor("d_av", [128, DC, NT], BF16, kind="ExternalOutput").ap()
        dbg["mix"] = nc.dram_tensor("d_mix", [128, DC, NT], F32, kind="ExternalOutput").ap()
        dbg["xn2"] = nc.dram_tensor("d_xn2", [128, DC, NT], BF16, kind="ExternalOutput").ap()
        dbg["h"] = nc.dram_tensor("d_h", [128, ICN, NT], BF16, kind="ExternalOutput").ap()
        dbg["ffn"] = nc.dram_tensor("d_ffn", [2, 128, DC, 512], F32, kind="ExternalOutput").ap()
        dbg["kin"] = nc.dram_tensor("d_kin", [2048, 1024], BF16, kind="ExternalOutput").ap()

    cag_in = nc.dram_tensor("cag_in", [4096, 1024], BF16).ap()
    cag_out = nc.dram_tensor("cag_out", [4, 4096, 1024], BF16).ap()
    kag_in = cag_in[0:2048, :]
    vag_in = cag_in[2048:4096, :].rearrange("(t two) c -> t (two c)", two=2)
    kag_out = cag_out[:, 0:2048, :]
    vag_out_of = [cag_out[rho, 2048:4096, :].rearrange("(t two) c -> t (two c)", two=2)
                  for rho in range(4)]
    x1d = nc.dram_tensor("x1d", [DC, 128, NT], F32).ap()

    ctr = {"X": 0, "SC": 0, "W": 0, "ST": 0, "CC": 0, "TE": 0, "ACT": 0, "DVE": 0}
    bank_free = [None] * 8

    with (
        nc.semaphore("sX") as sX,
        nc.semaphore("sSC") as sSC,
        nc.semaphore("sW") as sW,
        nc.semaphore("sST") as sST,
        nc.semaphore("sCC") as sCC,
        nc.semaphore("sTE") as sTE,
        nc.semaphore("sACT") as sACT,
        nc.semaphore("sDVE") as sDVE,
        nc.Block() as block,
    ):
        SEMS = {"X": sX, "SC": sSC, "W": sW, "ST": sST, "CC": sCC,
                "TE": sTE, "ACT": sACT, "DVE": sDVE}

        def mark(name, n=1):
            ctr[name] += n
            return ctr[name]

        def w(eng, name, val):
            if val > 0:
                eng.wait_ge(SEMS[name], val)

        def te_bank_wait(te, i):
            if bank_free[i] is not None:
                nm, v = bank_free[i]
                w(te, nm, v)
                bank_free[i] = None

        guards = []

        def alloc(name, shape, dtype, side=None):
            g = nc.sbuf_tensor(name, shape, dtype, side=side)
            t = g.__enter__()
            return g, t

        ones_g, ones_bf = alloc("ones_bf", [128, 128], BF16, side="left")
        onesf_g, ones_f = alloc("ones_f", [128, 128], F32, side="left")
        stats_g, stats = alloc("stats", [128, 1024], F32, side="left")
        negm_g, negm_b = alloc("negm_b", [128, 1024], F32, side="left")
        rstd_g, rstd_b = alloc("rstd_b", [128, 1024], F32, side="left")
        qbg, qb_sb = alloc("qb_sb", [128, 16], F32, side="left")
        kbg, kb_sb = alloc("kb_sb", [128, 16], F32, side="left")
        obg, ob_sb = alloc("ob_sb", [128, 16], F32, side="left")
        vbg, vb_sb = alloc("vb_sb", [1, 2048], BF16, side="left")
        mkg, mask_sb = alloc("mask_sb", [128, 4, 128], BF16, side="left")
        fcg, fc = alloc("fconsts", [128, 8], F32, side="left")
        guards += [ones_g, onesf_g, stats_g, negm_g, rstd_g, qbg, kbg, obg,
                   vbg, mkg, fcg]
        psg = [nc.psum_tensor(f"pb{i}", [128, 512], F32) for i in range(8)]
        pb = [g.__enter__() for g in psg]

        def p0_sync(sync):
            sync.dma_start(out=qb_sb[:, :], in_=qb).then_inc(sX, 16)
            sync.dma_start(out=kb_sb[:, :], in_=kb).then_inc(sX, 16)
            sync.dma_start(out=ob_sb[:, :], in_=ob).then_inc(sX, 16)
            sync.dma_start(out=mask_sb[:, :, :], in_=masks).then_inc(sX, 16)
        block.sync(p0_sync)
        mark("X", 4)
        v_p0 = ctr["X"] * 16

        def p0_gp(g):
            g.dma_start(out=vb_sb[:, :], in_=vb).then_inc(sW, 16)
        block.gpsimd(p0_gp)
        mark("W")
        v_vb = ctr["W"] * 16

        def p0_vec(v):
            v.memset(ones_bf[:, :], 1.0)
            v.memset(ones_f[:, :], 1.0)
            v.memset(fc[:, 0:1], -CEXP)
            v.memset(fc[:, 1:2], SCALE)
            v.memset(fc[:, 2:3], -0.5).then_inc(sDVE, 1)
        block.vector(p0_vec)
        v_consts = mark("DVE")

        def ln_stats(src, n_tok, xbF, sqF, ready_wait=None):
            nh = n_tok // 512

            def vec(v):
                if ready_wait is not None:
                    nm, val = ready_wait
                    w(v, nm, val)
                ins = None
                for dc in range(DC):
                    v.tensor_copy(xbF[:, dc, 0:n_tok], src(dc))
                    ins = v.tensor_tensor(sqF[:, dc, 0:n_tok], xbF[:, dc, 0:n_tok],
                                          xbF[:, dc, 0:n_tok], ALU.mult)
                ins.then_inc(sDVE, 1)
            block.vector(vec)
            v_prep = mark("DVE")

            def te(t):
                w(t, "DVE", v_prep)
                for i in range(2 * nh):
                    te_bank_wait(t, 4 + i)
                ins = None
                for gi in range(2 * nh):
                    buf = xbF if gi < nh else sqF
                    hh = gi % nh
                    for dc in range(DC):
                        ins = t.matmul(pb[4 + gi][:, 0:512], ones_bf[:, :],
                                       buf[:, dc, 512 * hh:512 * hh + 512],
                                       start=(dc == 0), stop=(dc == DC - 1))
                ins.then_inc(sTE, 1)
            block.tensor(te)
            v_stats = mark("TE")

            def fin_vec(v):
                w(v, "TE", v_stats)
                ins = None
                for h in range(nh):
                    sl = slice(512 * h, 512 * h + 512)
                    v.tensor_scalar(negm_b[:, sl], pb[4 + h][:, 0:512], -RDINV, None, ALU.mult)
                    v.tensor_scalar(stats[:, sl], pb[4 + nh + h][:, 0:512], RDINV, EPS,
                                    ALU.mult, ALU.add)
                    v.tensor_tensor(rstd_b[:, sl], negm_b[:, sl], negm_b[:, sl], ALU.mult)
                    ins = v.tensor_tensor(stats[:, sl], stats[:, sl], rstd_b[:, sl], ALU.subtract)
                ins.then_inc(sDVE, 1)
            block.vector(fin_vec)
            v_fin = mark("DVE")
            for i in range(2 * nh):
                bank_free[4 + i] = ("DVE", v_fin)

            def fin_act(s):
                w(s, "DVE", v_fin)
                ins = None
                for h in range(nh):
                    sl = slice(512 * h, 512 * h + 512)
                    s.activation(stats[:, sl], stats[:, sl], AF.Ln)
                    ins = s.activation(rstd_b[:, sl], stats[:, sl], AF.Exp, scale=fc[:, 2:3])
                ins.then_inc(sACT, 1)
            block.scalar(fin_act)
            return mark("ACT")

        def ln_apply(src, dst, n_tok, tmp, v_rstd):
            dve_m = [0] * DC

            def vec(v):
                w(v, "ACT", v_rstd)
                for dc in range(DC):
                    v.tensor_tensor(tmp[:, 0:n_tok], src(dc), negm_b[:, 0:n_tok], ALU.add)
                    v.tensor_tensor(dst(dc), tmp[:, 0:n_tok], rstd_b[:, 0:n_tok],
                                    ALU.mult).then_inc(sDVE, 1)
                    dve_m[dc] = mark("DVE")
            block.vector(vec)
            return dve_m

        xn1_g, xn1 = alloc("xn1", [128, DC, NT], BF16, side="left")
        x_g, x_sb = alloc("x_sb", [128, DC, NT], F32, side="right")
        xbF_g, xbF = alloc("xbF", [128, DC, NT], BF16, side="right")
        sqF_g, sqF = alloc("sqF", [128, DC, NT], BF16, side="right")
        tmp_g, tmp32 = alloc("tmp32", [128, NT], F32, side="right")

        def p1_sync(sync):
            sync.dma_start(out=x_sb[:, :, :],
                           in_=xT.rearrange("(dc p) t -> p dc t", p=128)).then_inc(sX, 16)
        block.sync(p1_sync)
        mark("X")
        v_xload = ctr["X"] * 16

        v_rstd1 = ln_stats(lambda dc: x_sb[:, dc, :], NT, xbF, sqF,
                           ready_wait=("X", v_xload))
        ln1_m = ln_apply(lambda dc: x_sb[:, dc, :], lambda dc: xn1[:, dc, :], NT,
                         tmp32, v_rstd1)
        v_xn1 = ln1_m[-1]
        tmp_g.__exit__(None, None, None)
        sqF_g.__exit__(None, None, None)
        xbF_g.__exit__(None, None, None)
        x_g.__exit__(None, None, None)

        q_g, q_sb = alloc("q_sb", [128, DC, NT], BF16, side="right")
        wqF_g, wqF = alloc("wqF", [128, DC, 2048], BF16, side="right")
        wvh_g, wvh = alloc("wvh", [128, DC, 1024], BF16, side="right")
        vt_g = [nc.sbuf_tensor(f"vt{i}", [128, 512], BF16, side="right") for i in range(2)]
        vt = [g.__enter__() for g in vt_g]

        n_vtile = 0
        sc_base_v = ctr["SC"]
        v_half_te = 0
        for half in range(2):
            def p2a_gp(g, half=half, v_half_te=v_half_te):
                if half == 0:
                    w(g, "DVE", v_xn1)
                if half == 1:
                    w(g, "TE", v_half_te)
                for dc in range(DC):
                    g.dma_start(out=wvh[:, dc, :],
                                in_=wv[dc, :, 1024 * half:1024 * half + 1024]).then_inc(sW, 16)
                    mark("W")
            block.gpsimd(p2a_gp)
            v_wvh = ctr["W"] * 16

            for tc in range(8):
                for og in range(2):
                    o0 = 1024 * half + 512 * og
                    bk = (tc * 2 + og) % 4
                    v_mm = [0]

                    def p2a_te(t, half=half, tc=tc, og=og, bk=bk, v_wvh=v_wvh, v_mm=v_mm):
                        if tc == 0 and og == 0:
                            w(t, "W", v_wvh)
                            w(t, "W", v_vb)
                            w(t, "DVE", v_xn1)
                        te_bank_wait(t, bk)
                        for dc in range(DC):
                            t.matmul(pb[bk][:, 0:512], xn1[:, dc, 128 * tc:128 * tc + 128],
                                     wvh[:, dc, 512 * og:512 * og + 512],
                                     start=(dc == 0), stop=False)
                        t.matmul(pb[bk][:, 0:512], ones_bf[0:1, :],
                                 vb_sb[0:1, o0:o0 + 512], start=False,
                                 stop=True).then_inc(sTE, 1)
                        v_mm[0] = mark("TE")
                    block.tensor(p2a_te)

                    def p2a_sc(s, tc=tc, og=og, o0=o0, bk=bk, v_mm=v_mm, n=n_vtile):
                        w(s, "TE", v_mm[0])
                        if n >= 2:
                            w(s, "SC", (sc_base_v + n - 1) * 16)
                        bf = n % 2
                        s.activation(vt[bf][:, :], pb[bk][:, 0:512],
                                     AF.Identity).then_inc(sACT, 1)
                        w(s, "ACT", mark("ACT"))
                        s.dma_start(out=vag_in[128 * tc:128 * tc + 128, o0:o0 + 512],
                                    in_=vt[bf][:, :]).then_inc(sSC, 16)
                    block.scalar(p2a_sc)
                    bank_free[bk] = ("ACT", ctr["ACT"])
                    mark("SC")
                    n_vtile += 1
            v_half_te = ctr["TE"]
        v_vag_stores = ctr["SC"] * 16
        for g in reversed(vt_g):
            g.__exit__(None, None, None)
        wvh_g.__exit__(None, None, None)

        def p2b_wqpre(g):
            w(g, "DVE", v_xn1)
            for oc in range(16):
                g.dma_start(out=wqF[:, oc, :], in_=wq[oc, :, :]).then_inc(sW, 16)
                mark("W")
        block.gpsimd(p2b_wqpre)
        v_wqpre = ctr["W"] * 16

        wb_g = [nc.sbuf_tensor(f"wb{i}", [128, 2048], BF16, side="right") for i in range(2)]
        wb = [g.__enter__() for g in wb_g]
        wf_g = [nc.sbuf_tensor(f"wf{i}", [128, 2048], F32, side="right") for i in range(2)]
        wf32 = [g.__enter__() for g in wf_g]
        kt_g = [nc.sbuf_tensor(f"kt{i}", [128, 512], BF16, side="right") for i in range(2)]
        kt = [g.__enter__() for g in kt_g]

        def proj_fm(wsrc, bias_sb, is_k, first_gp_waits=(), resident=None):
            te_oc = [0] * 16
            base_x = ctr["X"]
            n_tile = 0
            sc_base = ctr["SC"]
            cast_m = [0] * 16
            for oc in range(16):
                if resident is None:
                    def gp(sync, oc=oc):
                        if oc == 0:
                            for nm, val in first_gp_waits:
                                w(sync, nm, val)
                        if oc >= 2:
                            w(sync, "ACT", cast_m[oc - 2])
                        sync.dma_start(out=wf32[oc % 2][:, :], in_=wsrc[oc, :, :]).then_inc(sX, 16)
                        mark("X")
                    block.sync(gp)

                    def sc_cast(s, oc=oc):
                        w(s, "X", (base_x + oc + 1) * 16)
                        if oc >= 2:
                            w(s, "TE", te_oc[oc - 2])
                        s.activation(wb[oc % 2][:, :], wf32[oc % 2][:, :],
                                     AF.Identity).then_inc(sACT, 1)
                        cast_m[oc] = mark("ACT")
                    block.scalar(sc_cast)

                for tg in range(2):
                    bk = 4 + (oc * 2 + tg) % 4
                    v_mm = [0]

                    def te(t, oc=oc, tg=tg, bk=bk, v_mm=v_mm):
                        if oc == 0 and tg == 0:
                            w(t, "DVE", v_xn1)
                            if resident is not None:
                                w(t, "W", resident[1])
                        if tg == 0 and resident is None:
                            w(t, "ACT", cast_m[oc])
                        te_bank_wait(t, bk)
                        wsb = wb[oc % 2] if resident is None else resident[0][:, oc, :]
                        ins = None
                        for dc in range(DC):
                            ins = t.matmul(pb[bk][:, 0:512],
                                           wsb[:, 128 * dc:128 * dc + 128],
                                           xn1[:, dc, 512 * tg:512 * tg + 512],
                                           start=(dc == 0), stop=(dc == DC - 1))
                        ins.then_inc(sTE, 1)
                        v_mm[0] = mark("TE")
                        if tg == 1:
                            te_oc[oc] = ctr["TE"]
                    block.tensor(te)

                    def sc(s, oc=oc, tg=tg, bk=bk, v_mm=v_mm, n=n_tile):
                        if oc == 0 and tg == 0:
                            w(s, "X", v_p0)
                        w(s, "TE", v_mm[0])
                        if is_k:
                            if n >= 2:
                                w(s, "SC", (sc_base + n - 1) * 16)
                            bf = n % 2
                            s.activation(kt[bf][:, :], pb[bk][:, 0:512], AF.Identity,
                                         bias=bias_sb[:, oc:oc + 1]).then_inc(sACT, 1)
                            w(s, "ACT", mark("ACT"))
                            s.dma_start(out=kag_in[128 * oc:128 * oc + 128,
                                                   512 * tg:512 * tg + 512],
                                        in_=kt[bf][:, :]).then_inc(sSC, 16)
                        else:
                            s.activation(q_sb[:, oc, 512 * tg:512 * tg + 512],
                                         pb[bk][:, 0:512], AF.Identity,
                                         bias=bias_sb[:, oc:oc + 1]).then_inc(sACT, 1)
                            mark("ACT")
                    block.scalar(sc)
                    bank_free[bk] = ("ACT", ctr["ACT"])
                    if is_k:
                        mark("SC")
                    n_tile += 1
            return cast_m

        k_cast = proj_fm(wk, kb_sb, True,
                         first_gp_waits=(("SC", v_vag_stores), ("TE", v_half_te)))
        v_kag_stores = ctr["SC"] * 16

        def p2b_cc(g):
            w(g, "SC", v_kag_stores)
            g.collective_compute(
                "AllGather", ALU.bypass,
                replica_groups=[[0, 1, 2, 3], [4, 5, 6, 7]],
                ins=[cag_in.opt()], outs=[cag_out.opt()],
            ).then_inc(sCC, 1)
        block.gpsimd(p2b_cc)
        mark("CC", 1)

        proj_fm(wq, qb_sb, False, resident=(wqF, v_wqpre))
        v_q_done = ctr["ACT"]

        if DEBUG:
            def dbg_kin(sync):
                w(sync, "SC", v_kag_stores)
                sync.dma_start(out=dbg["kin"], in_=kag_in).then_inc(sST, 16)
            block.sync(dbg_kin)
            mark("ST")

            def dbg_p3(sync):
                w(sync, "DVE", v_xn1)
                sync.dma_start(out=dbg["xn1"], in_=xn1[:, :, :]).then_inc(sST, 16)
                w(sync, "ACT", v_q_done)
                sync.dma_start(out=dbg["q"], in_=q_sb[:, :, :]).then_inc(sST, 16)
                w(sync, "CC", 1)
                sync.dma_start(out=dbg["kag"], in_=kag_out).then_inc(sST, 16)
                for rho in range(4):
                    sync.dma_start(out=dbg["vag"][rho], in_=vag_out_of[rho]).then_inc(sST, 16)
            block.sync(dbg_p3)
            mark("ST", 7)

        for g in reversed(kt_g):
            g.__exit__(None, None, None)
        for g in reversed(wf_g):
            g.__exit__(None, None, None)
        for g in reversed(wb_g):
            g.__exit__(None, None, None)
        wqF_g.__exit__(None, None, None)
        xn1_g.__exit__(None, None, None)

        av_g, av_all = alloc("av_all", [128, DC, NT], BF16, side="left")
        k2_g = [nc.sbuf_tensor(f"k2_{i}", [128, 2, 4, 1024], BF16, side="left") for i in range(2)]
        k2 = [g.__enter__() for g in k2_g]
        v2_g = [nc.sbuf_tensor(f"v2_{i}", [128, 4, 8, 256], BF16, side="left") for i in range(2)]
        v2 = [g.__enter__() for g in v2_g]
        rb_g, rb_sb = alloc("rb_sb", [128, 1024], F32, side="left")

        aoff = {}
        acur = 0
        for j in range(8):
            for kc in range(4):
                aoff[(j, kc)] = acur
                acur += 1024 - 128 * j
        ATOT = acur
        pta_g, pta = alloc("pta", [128, ATOT], BF16, side="left")

        parts = []
        for j in range(8):
            wj = 1024 - 128 * j
            for kc in range(4):
                parts.append((j, kc, 0, min(512, wj)))
                if wj > 512:
                    parts.append((j, kc, 512, wj - 512))
        npp = len(parts)

        act_base4 = ctr["ACT"]
        dve_base4 = ctr["DVE"]

        def exp_mark(h, n):
            return act_base4 + h * npp + n + 1

        def msk_mark(h, k):
            return dve_base4 + h * 33 + k + 1

        def evac_mark(h):
            return dve_base4 + h * 33 + 33

        SBANKS = [0, 1, 6, 7]
        kv_m = [0] * 8
        hp_te_done = [0] * 8
        groups_mark = {-1: 0}

        for hp in range(8):
            def p4_sync_hp(sync, hp=hp):
                if hp == 0:
                    w(sync, "CC", 1)
                    w(sync, "ACT", v_q_done)
                if hp >= 2:
                    w(sync, "TE", hp_te_done[hp - 2])
                b = hp % 2
                for rho in range(4):
                    sync.dma_start(
                        out=k2[b][:, :, rho, :],
                        in_=kag_out[rho, 256 * hp:256 * hp + 256, :]
                        .rearrange("(e p) c -> p e c", p=128)).then_inc(sX, 16)
                    mark("X")
                    sync.dma_start(
                        out=v2[b][:, rho, :, :],
                        in_=vag_out_of[rho][:, 256 * hp:256 * hp + 256]
                        .rearrange("(g p) c -> p g c", p=128)).then_inc(sX, 16)
                    mark("X")
                kv_m[hp] = ctr["X"] * 16
            block.sync(p4_sync_hp)

            for e in range(2):
                h = 2 * hp + e
                b = hp % 2
                s_m = [0] * npp

                def p4_te_scores(t, hp=hp, e=e, h=h, b=b, s_m=s_m):
                    if h == 0:
                        for i in range(8):
                            te_bank_wait(t, i)
                        w(t, "DVE", v_consts)
                    w(t, "X", kv_m[hp])
                    for n, (j, kc, plo, pn) in enumerate(parts):
                        if n >= 4:
                            w(t, "ACT", exp_mark(h, n - 4))
                        t.matmul(pb[SBANKS[n % 4]][:, 0:pn],
                                 k2[b][:, e, kc, 128 * j:128 * j + 128],
                                 q_sb[:, h, 128 * j + plo:128 * j + plo + pn],
                                 start=True, stop=True).then_inc(sTE, 1)
                        s_m[n] = mark("TE")
                block.tensor(p4_te_scores)

                def p4_act(s, h=h, s_m=s_m):
                    if h == 0:
                        w(s, "DVE", v_consts)
                    w(s, "TE", groups_mark[h - 1])
                    for n, (j, kc, plo, pn) in enumerate(parts):
                        w(s, "TE", s_m[n])
                        s.activation(pta[:, aoff[(j, kc)] + plo:aoff[(j, kc)] + plo + pn],
                                     pb[SBANKS[n % 4]][:, 0:pn],
                                     AF.Exp, bias=fc[:, 0:1], scale=fc[:, 1:2]).then_inc(sACT, 1)
                        assert mark("ACT") == exp_mark(h, n)
                block.scalar(p4_act)
                v_exp_last = exp_mark(h, npp - 1)
                v_exp_A = exp_mark(h, max(n for n, p_ in enumerate(parts)
                                          if p_[0] <= 3 and p_[2] == 0))
                v_msk_A = msk_mark(h, 15)

                def p4_dve(v, h=h):
                    if h == 0:
                        w(v, "X", v_p0)
                    w(v, "TE", groups_mark[h - 1])
                    kk = 0
                    for j in range(8):
                        for kc in range(4):
                            n = parts.index((j, kc, 0, min(512, 1024 - 128 * j)))
                            w(v, "ACT", exp_mark(h, n))
                            o0 = aoff[(j, kc)]
                            v.tensor_tensor(pta[:, o0:o0 + 128], pta[:, o0:o0 + 128],
                                            mask_sb[:, kc, :], ALU.mult).then_inc(sDVE, 1)
                            assert mark("DVE") == msk_mark(h, kk)
                            kk += 1
                block.vector(p4_dve)
                v_msk_last = msk_mark(h, 31)

                def p4_te_groups(t, h=h, b=b, e=e):
                    if h > 0:
                        w(t, "DVE", evac_mark(h - 1))
                    w(t, "ACT", v_exp_A)
                    w(t, "DVE", v_msk_A)
                    for j in range(4):
                        for kc in range(4):
                            o0 = aoff[(j, kc)]
                            nA = 512 - 128 * j
                            t.matmul(pb[2][:, 128 * j:512],
                                     v2[b][:, kc, j, 128 * e:128 * e + 128],
                                     pta[:, o0:o0 + nA],
                                     start=(j == 0 and kc == 0),
                                     stop=(j == 3 and kc == 3))
                    for j in range(4):
                        for kc in range(4):
                            o0 = aoff[(j, kc)]
                            nA = 512 - 128 * j
                            t.matmul(pb[4][:, 128 * j:512], ones_bf[:, :],
                                     pta[:, o0:o0 + nA],
                                     start=(j == 0 and kc == 0),
                                     stop=(j == 3 and kc == 3))
                    w(t, "ACT", v_exp_last)
                    w(t, "DVE", v_msk_last)
                    for j in range(8):
                        for kc in range(4):
                            o0 = aoff[(j, kc)]
                            if j <= 3:
                                lo, q0 = 512 - 128 * j, 0
                            else:
                                lo, q0 = 0, 128 * j - 512
                            nB = 512 - q0
                            t.matmul(pb[3][:, q0:512],
                                     v2[b][:, kc, j, 128 * e:128 * e + 128],
                                     pta[:, o0 + lo:o0 + lo + nB],
                                     start=(j == 0 and kc == 0),
                                     stop=(j == 7 and kc == 3))
                    ins = None
                    for j in range(8):
                        for kc in range(4):
                            o0 = aoff[(j, kc)]
                            if j <= 3:
                                lo, q0 = 512 - 128 * j, 0
                            else:
                                lo, q0 = 0, 128 * j - 512
                            nB = 512 - q0
                            ins = t.matmul(pb[5][:, q0:512], ones_bf[:, :],
                                           pta[:, o0 + lo:o0 + lo + nB],
                                           start=(j == 0 and kc == 0),
                                           stop=(j == 7 and kc == 3))
                    ins.then_inc(sTE, 1)
                block.tensor(p4_te_groups)
                v_av_te = mark("TE")
                groups_mark[h] = v_av_te
                hp_te_done[hp] = v_av_te

                def p4_evac(v, h=h, v_av_te=v_av_te):
                    w(v, "TE", v_av_te)
                    v.reciprocal(rb_sb[:, 0:512], pb[4][:, 0:512])
                    v.reciprocal(rb_sb[:, 512:1024], pb[5][:, 0:512])
                    v.tensor_tensor(av_all[:, h, 0:512], pb[2][:, 0:512],
                                    rb_sb[:, 0:512], ALU.mult)
                    v.tensor_tensor(av_all[:, h, 512:1024], pb[3][:, 0:512],
                                    rb_sb[:, 512:1024], ALU.mult).then_inc(sDVE, 1)
                    assert mark("DVE") == evac_mark(h)
                block.vector(p4_evac)
        for i in (2, 3, 4, 5):
            bank_free[i] = ("DVE", evac_mark(15))
        for q_ in range(4):
            last_n = max(n for n in range(npp) if n % 4 == q_)
            bank_free[SBANKS[q_]] = ("ACT", exp_mark(15, last_n))
        v_av_done = ctr["DVE"]

        if DEBUG:
            def dbg_p4(sync):
                w(sync, "DVE", v_av_done)
                sync.dma_start(out=dbg["av"], in_=av_all[:, :, :]).then_inc(sST, 16)
            block.sync(dbg_p4)
            mark("ST")

        pta_g.__exit__(None, None, None)
        rb_g.__exit__(None, None, None)
        for g in reversed(v2_g):
            g.__exit__(None, None, None)
        for g in reversed(k2_g):
            g.__exit__(None, None, None)
        q_g.__exit__(None, None, None)

        mix_g, attnmix = alloc("attnmix", [128, DC, NT], F32, side="right")
        wb2_g = [nc.sbuf_tensor(f"wo{i}", [128, 2048], BF16, side="right") for i in range(2)]
        wb2 = [g.__enter__() for g in wb2_g]
        wf5_g = [nc.sbuf_tensor(f"wf5{i}", [128, 2048], F32, side="right") for i in range(2)]
        wf5 = [g.__enter__() for g in wf5_g]

        te_oc5 = [0] * 16
        base_x5 = ctr["X"]
        cast5_m = [0] * 16
        for oc in range(16):
            def p5_gp(sync, oc=oc):
                if oc == 0:
                    w(sync, "DVE", v_av_done)
                if oc >= 2:
                    w(sync, "ACT", cast5_m[oc - 2])
                sync.dma_start(out=wf5[oc % 2][:, :], in_=wo[oc, :, :]).then_inc(sX, 16)
                mark("X")
            block.sync(p5_gp)

            def p5_cast(s, oc=oc):
                w(s, "X", (base_x5 + oc + 1) * 16)
                if oc >= 2:
                    w(s, "TE", te_oc5[oc - 2])
                s.activation(wb2[oc % 2][:, :], wf5[oc % 2][:, :],
                             AF.Identity).then_inc(sACT, 1)
                cast5_m[oc] = mark("ACT")
            block.scalar(p5_cast)

            for tg in range(2):
                bk = (oc * 2 + tg) % 4
                v_mm = [0]

                def p5_te(t, oc=oc, tg=tg, bk=bk, v_mm=v_mm):
                    if oc == 0 and tg == 0:
                        w(t, "DVE", v_av_done)
                    if tg == 0:
                        w(t, "ACT", cast5_m[oc])
                    te_bank_wait(t, bk)
                    ins = None
                    for dc in range(DC):
                        ins = t.matmul(pb[bk][:, 0:512],
                                       wb2[oc % 2][:, 128 * dc:128 * dc + 128],
                                       av_all[:, dc, 512 * tg:512 * tg + 512],
                                       start=(dc == 0), stop=(dc == DC - 1))
                    ins.then_inc(sTE, 1)
                    v_mm[0] = mark("TE")
                    if tg == 1:
                        te_oc5[oc] = ctr["TE"]
                block.tensor(p5_te)

                def p5_sc(s, oc=oc, tg=tg, bk=bk, v_mm=v_mm):
                    w(s, "TE", v_mm[0])
                    s.activation(attnmix[:, oc, 512 * tg:512 * tg + 512], pb[bk][:, 0:512],
                                 AF.Identity, bias=ob_sb[:, oc:oc + 1]).then_inc(sACT, 1)
                block.scalar(p5_sc)
                bank_free[bk] = ("ACT", mark("ACT"))
        v_mix = ctr["ACT"]
        v_te_p5 = ctr["TE"]

        if DEBUG:
            def dbg_p5(sync):
                w(sync, "ACT", v_mix)
                sync.dma_start(out=dbg["mix"], in_=attnmix[:, :, :]).then_inc(sST, 16)
            block.sync(dbg_p5)
            mark("ST")

        for g in reversed(wf5_g):
            g.__exit__(None, None, None)
        for g in reversed(wb2_g):
            g.__exit__(None, None, None)
        av_g.__exit__(None, None, None)

        xn2_g, xn2 = alloc("xn2", [128, DC, NT], BF16, side="left")
        xbF2_g, xbF2 = alloc("xbF2", [128, DC, NT], BF16, side="right")
        sqF2_g, sqF2 = alloc("sqF2", [128, DC, NT], BF16, side="right")
        xs_g = [nc.sbuf_tensor(f"xs{i}", [128, NT], F32, side="right") for i in range(4)]
        xs = [g.__enter__() for g in xs_g]
        tmp2_g, tmp2 = alloc("tmp2", [128, NT], F32, side="right")

        v_rstd2 = ln_stats(lambda dc: attnmix[:, dc, :], NT, xbF2, sqF2,
                           ready_wait=("ACT", v_mix))

        x_dve_m = [0] * DC
        xs_loaded = [0] * DC
        spill_m = [0] * DC
        for dc in range(DC):
            def p6_load(sync, dc=dc):
                if dc == 0:
                    w(sync, "TE", v_te_p5)
                if dc >= 4:
                    w(sync, "DVE", x_dve_m[dc - 4])
                sync.dma_start(out=xs[dc % 4][:, :],
                               in_=xT.rearrange("(dc p) t -> p dc t", p=128)[:, dc, :]
                               ).then_inc(sX, 16)
                mark("X")
                xs_loaded[dc] = ctr["X"] * 16
            block.sync(p6_load)

            def p6_app(v, dc=dc):
                if dc == 0:
                    w(v, "ACT", v_rstd2)
                w(v, "X", xs_loaded[dc])
                v.tensor_tensor(tmp2[:, :], attnmix[:, dc, :], negm_b[:, :], ALU.add)
                v.tensor_tensor(tmp2[:, :], tmp2[:, :], rstd_b[:, :], ALU.mult)
                v.tensor_tensor(attnmix[:, dc, :], tmp2[:, :], xs[dc % 4][:, :],
                                ALU.add).then_inc(sDVE, 1)
                x_dve_m[dc] = mark("DVE")
            block.vector(p6_app)

            def p6_spill(sync, dc=dc):
                w(sync, "DVE", x_dve_m[dc])
                sync.dma_start(out=x1d[dc, :, :], in_=attnmix[:, dc, :]).then_inc(sX, 16)
                mark("X")
                spill_m[dc] = ctr["X"] * 16
            block.sync(p6_spill)

        v_rstd3 = ln_stats(lambda dc: attnmix[:, dc, :], NT, xbF2, sqF2,
                           ready_wait=("DVE", x_dve_m[-1]))
        ln3_m = ln_apply(lambda dc: attnmix[:, dc, :], lambda dc: xn2[:, dc, :], NT,
                         tmp2, v_rstd3)
        v_xn2 = ln3_m[-1]

        if DEBUG:
            def dbg_p6(sync):
                w(sync, "DVE", v_xn2)
                sync.dma_start(out=dbg["xn2"], in_=xn2[:, :, :]).then_inc(sST, 16)
            block.sync(dbg_p6)
            mark("ST")

        tmp2_g.__exit__(None, None, None)
        for g in reversed(xs_g):
            g.__exit__(None, None, None)
        sqF2_g.__exit__(None, None, None)
        xbF2_g.__exit__(None, None, None)
        mix_g.__exit__(None, None, None)

        h_g, h_sb = alloc("h_sb", [128, ICN, NT], BF16, side="right")
        gu_g = [nc.sbuf_tensor(f"gu{i}", [128, 2048], BF16, side="right") for i in range(4)]
        gub = [g.__enter__() for g in gu_g]
        gt_g = [nc.sbuf_tensor(f"gt{i}", [128, 512], BF16, side="right") for i in range(4)]
        gt = [g.__enter__() for g in gt_g]

        w7 = [0] * ICN
        te7 = {}
        dve7 = {}
        act7 = {}
        for ic in range(ICN):
            def p7_gp(g, ic=ic):
                if ic == 0:
                    w(g, "DVE", v_xn2)
                if ic >= 2:
                    w(g, "DVE", dve7[(ic - 2, 1)])
                g.dma_start(out=gub[ic % 2][:, :], in_=wgu[ic, :, :]).then_inc(sW, 16)
                mark("W")
                g.dma_start(out=gub[2 + ic % 2][:, :], in_=wgu[64 + ic, :, :]).then_inc(sW, 16)
                mark("W")
                w7[ic] = ctr["W"] * 16
            block.gpsimd(p7_gp)

            def p7_te(t, ic=ic):
                if ic == 0:
                    w(t, "DVE", v_xn2)
                    for i in range(8):
                        te_bank_wait(t, i)
                w(t, "W", w7[ic])
                if ic >= 2:
                    w(t, "ACT", act7[(ic - 2, 1)])
                    w(t, "DVE", dve7[(ic - 2, 1)])
                for tg in range(2):
                    bg = (ic % 2) * 4 + tg * 2
                    bu = bg + 1
                    for dc in range(DC):
                        t.matmul(pb[bg][:, 0:512], gub[ic % 2][:, 128 * dc:128 * dc + 128],
                                 xn2[:, dc, 512 * tg:512 * tg + 512],
                                 start=(dc == 0), stop=(dc == DC - 1))
                    ins = None
                    for dc in range(DC):
                        ins = t.matmul(pb[bu][:, 0:512],
                                       gub[2 + ic % 2][:, 128 * dc:128 * dc + 128],
                                       xn2[:, dc, 512 * tg:512 * tg + 512],
                                       start=(dc == 0), stop=(dc == DC - 1))
                    ins.then_inc(sTE, 1)
                    te7[(ic, tg)] = mark("TE")
            block.tensor(p7_te)

            def p7_sc(s, ic=ic):
                for tg in range(2):
                    bg = (ic % 2) * 4 + tg * 2
                    w(s, "TE", te7[(ic, tg)])
                    if ic >= 2:
                        w(s, "DVE", dve7[(ic - 2, tg)])
                    s.activation(gt[(ic % 2) * 2 + tg][:, :], pb[bg][:, 0:512],
                                 AF.Silu).then_inc(sACT, 1)
                    act7[(ic, tg)] = mark("ACT")
            block.scalar(p7_sc)

            def p7_dve(v, ic=ic):
                if ic == 0:
                    w(v, "X", spill_m[-1])
                for tg in range(2):
                    bu = (ic % 2) * 4 + tg * 2 + 1
                    w(v, "ACT", act7[(ic, tg)])
                    v.tensor_tensor(h_sb[:, ic, 512 * tg:512 * tg + 512],
                                    gt[(ic % 2) * 2 + tg][:, :], pb[bu][:, 0:512],
                                    ALU.mult).then_inc(sDVE, 1)
                    dve7[(ic, tg)] = mark("DVE")
            block.vector(p7_dve)
        for i in range(8):
            ic_last = ICN - 2 + (i // 4)
            tg_last = (i % 4) // 2
            if i % 2 == 0:
                bank_free[i] = ("ACT", act7[(ic_last, tg_last)])
            else:
                bank_free[i] = ("DVE", dve7[(ic_last, tg_last)])

        for g in reversed(gt_g):
            g.__exit__(None, None, None)
        for g in reversed(gu_g):
            g.__exit__(None, None, None)
        xn2_g.__exit__(None, None, None)

        wd_g = [nc.sbuf_tensor(f"wdb{i}", [128, 4096], BF16, side="right") for i in range(2)]
        wdb = [g.__enter__() for g in wd_g]
        fo_g, fo = alloc("ffnout", [128, DC, 512], F32, side="right")
        sq8_g, sq8 = alloc("sq8", [128, 8, 512], BF16, side="right")
        x1r_g = [nc.sbuf_tensor(f"x1r{i}", [128, 512], F32, side="right") for i in range(2)]
        x1r = [g.__enter__() for g in x1r_g]
        tmp4 = stats

        v_spill_all = spill_m[-1]
        wd_seq_te = []
        v_out_prev = [0]
        v_dump = [0]

        for tg in range(2):
            w8 = {}
            te8h = {}
            act8 = [0] * 16
            dve8 = [0] * 16

            for oc in range(16):
                def p8_gp(g, tg=tg, oc=oc):
                    if tg == 0 and oc == 0:
                        w(g, "TE", te7[(ICN - 1, 1)])
                    for hf in range(2):
                        nseq = len(wd_seq_te)
                        if nseq >= 2:
                            mrk = wd_seq_te[nseq - 2]
                            if mrk[0]:
                                w(g, "TE", mrk[0])
                        g.dma_start(out=wdb[(oc * 2 + hf) % 2][:, :],
                                    in_=wd[oc, :, 4096 * hf:4096 * hf + 4096]).then_inc(sW, 16)
                        mark("W")
                        w8[(oc, hf)] = ctr["W"] * 16
                        wd_seq_te.append([0])
                block.gpsimd(p8_gp)

                bk0 = (oc % 2) * 2
                bk1 = bk0 + 1

                def p8_te(t, tg=tg, oc=oc, bk0=bk0, bk1=bk1):
                    if tg == 0 and oc == 0:
                        w(t, "DVE", dve7[(ICN - 1, 1)])
                    te_bank_wait(t, bk0)
                    te_bank_wait(t, bk1)
                    for hf in range(2):
                        bk = bk0 if hf == 0 else bk1
                        w(t, "W", w8[(oc, hf)])
                        ins = None
                        for ic2 in range(32):
                            ic = 32 * hf + ic2
                            ins = t.matmul(pb[bk][:, 0:512],
                                           wdb[(oc * 2 + hf) % 2][:, 128 * ic2:128 * ic2 + 128],
                                           h_sb[:, ic, 512 * tg:512 * tg + 512],
                                           start=(ic2 == 0), stop=(ic2 == 31))
                        ins.then_inc(sTE, 1)
                        wd_seq_te[tg * 32 + oc * 2 + hf][0] = mark("TE")
                    te8h[oc] = ctr["TE"]
                block.tensor(p8_te)

                def p8_sc(s, tg=tg, oc=oc, bk0=bk0):
                    if tg == 1 and oc == 0:
                        w(s, "ST", v_out_prev[0])
                    w(s, "TE", te8h[oc])
                    s.activation(fo[:, oc, :], pb[bk0][:, 0:512],
                                 AF.Identity).then_inc(sACT, 1)
                    act8[oc] = mark("ACT")
                block.scalar(p8_sc)
                bank_free[bk0] = ("ACT", act8[oc])

                def p8_dve(v, tg=tg, oc=oc, bk1=bk1):
                    w(v, "ACT", act8[oc])
                    v.tensor_tensor(fo[:, oc, :], fo[:, oc, :], pb[bk1][:, 0:512],
                                    ALU.add).then_inc(sDVE, 1)
                    dve8[oc] = mark("DVE")
                block.vector(p8_dve)
                bank_free[bk1] = ("DVE", dve8[oc])

            def p8_sx(t, tg=tg):
                w(t, "DVE", dve8[15])
                te_bank_wait(t, 4)
                te_bank_wait(t, 5)
                te_bank_wait(t, 6)
                ins = None
                for oc in range(16):
                    ins = t.matmul(pb[4][:, 0:512], ones_f[:, :], fo[:, oc, :],
                                   start=(oc == 0), stop=(oc == 15))
                ins.then_inc(sTE, 1)
            block.tensor(p8_sx)
            v_sx = mark("TE")

            sq_m = [0, 0]
            sxx_m = [0, 0]
            for grp in range(2):
                def p8_sq(v, tg=tg, grp=grp):
                    w(v, "DVE", dve8[15])
                    if grp == 1:
                        w(v, "TE", sxx_m[0])
                    ins = None
                    for i in range(8):
                        oc = 8 * grp + i
                        ins = v.tensor_tensor(sq8[:, i, :], fo[:, oc, :], fo[:, oc, :],
                                              ALU.mult)
                    ins.then_inc(sDVE, 1)
                    sq_m[grp] = mark("DVE")
                block.vector(p8_sq)

                def p8_sxx(t, tg=tg, grp=grp):
                    w(t, "DVE", sq_m[grp])
                    ins = None
                    for i in range(8):
                        ins = t.matmul(pb[5 + grp][:, 0:512], ones_bf[:, :], sq8[:, i, :],
                                       start=(i == 0), stop=(i == 7))
                    ins.then_inc(sTE, 1)
                    sxx_m[grp] = mark("TE")
                block.tensor(p8_sxx)

            def p8_fin_act0(s, tg=tg):
                w(s, "TE", sxx_m[1])
                s.activation(stats[:, 512:1024], pb[6][:, 0:512],
                             AF.Identity).then_inc(sACT, 1)
            block.scalar(p8_fin_act0)
            v_cp6 = mark("ACT")

            def p8_fin_vec(v, tg=tg):
                w(v, "TE", v_sx)
                w(v, "ACT", v_cp6)
                v.tensor_scalar(negm_b[:, 0:512], pb[4][:, 0:512], -RDINV, None, ALU.mult)
                v.tensor_tensor(stats[:, 0:512], pb[5][:, 0:512], stats[:, 512:1024], ALU.add)
                v.tensor_scalar(stats[:, 0:512], stats[:, 0:512], RDINV, EPS, ALU.mult, ALU.add)
                v.tensor_tensor(rstd_b[:, 0:512], negm_b[:, 0:512], negm_b[:, 0:512], ALU.mult)
                v.tensor_tensor(stats[:, 0:512], stats[:, 0:512], rstd_b[:, 0:512],
                                ALU.subtract).then_inc(sDVE, 1)
            block.vector(p8_fin_vec)
            v_fin4 = mark("DVE")
            bank_free[4] = ("DVE", v_fin4)
            bank_free[5] = ("DVE", v_fin4)
            bank_free[6] = ("ACT", v_cp6)

            def p8_fin_act(s, tg=tg, v_fin4=v_fin4):
                w(s, "DVE", v_fin4)
                s.activation(stats[:, 0:512], stats[:, 0:512], AF.Ln)
                s.activation(rstd_b[:, 0:512], stats[:, 0:512], AF.Exp,
                             scale=fc[:, 2:3]).then_inc(sACT, 1)
            block.scalar(p8_fin_act)
            v_rstd4 = mark("ACT")

            if DEBUG:
                def dbg_p8(sync, tg=tg):
                    if tg == 0:
                        w(sync, "DVE", dve7[(ICN - 1, 1)])
                        sync.dma_start(out=dbg["h"], in_=h_sb[:, :, :]).then_inc(sST, 16)
                    w(sync, "DVE", dve8[15])
                    sync.dma_start(out=dbg["ffn"][tg], in_=fo[:, :, :]).then_inc(sST, 16)
                block.sync(dbg_p8)
                mark("ST", 2 if tg == 0 else 1)
                v_dump[0] = ctr["ST"] * 16

            x1l = [0] * 16
            app8 = [0] * 16
            for oc in range(16):
                def p8_x1(sync, tg=tg, oc=oc):
                    if tg == 0 and oc == 0:
                        w(sync, "X", v_spill_all)
                        w(sync, "TE", te7[(ICN - 1, 1)])
                    if oc >= 2:
                        w(sync, "DVE", app8[oc - 2])
                    sync.dma_start(out=x1r[oc % 2][:, :],
                                   in_=x1d[oc, :, 512 * tg:512 * tg + 512]).then_inc(sX, 16)
                    mark("X")
                    x1l[oc] = ctr["X"] * 16
                block.sync(p8_x1)

                def p8_app(v, tg=tg, oc=oc, v_rstd4=v_rstd4):
                    if oc == 0:
                        if DEBUG:
                            w(v, "ST", v_dump[0])
                        w(v, "ACT", v_rstd4)
                    w(v, "X", x1l[oc])
                    v.tensor_tensor(tmp4[:, 0:512], fo[:, oc, :], negm_b[:, 0:512], ALU.add)
                    v.tensor_tensor(tmp4[:, 0:512], tmp4[:, 0:512], rstd_b[:, 0:512], ALU.mult)
                    v.tensor_tensor(fo[:, oc, :], tmp4[:, 0:512], x1r[oc % 2][:, :],
                                    ALU.add).then_inc(sDVE, 1)
                    app8[oc] = mark("DVE")
                block.vector(p8_app)

            def p8_out(sync, tg=tg, app8=app8):
                w(sync, "DVE", app8[15])
                sync.dma_start(
                    out=outT.rearrange("(dc p) t -> p dc t", p=128)[:, :, 512 * tg:512 * tg + 512],
                    in_=fo[:, :, :]).then_inc(sST, 16)
            block.sync(p8_out)
            mark("ST")
            v_out_prev[0] = ctr["ST"] * 16

        def p_final(sync):
            w(sync, "ST", ctr["ST"] * 16)
        block.sync(p_final)

        for g in reversed(x1r_g):
            g.__exit__(None, None, None)
        sq8_g.__exit__(None, None, None)
        fo_g.__exit__(None, None, None)
        for g in reversed(wd_g):
            g.__exit__(None, None, None)
        h_g.__exit__(None, None, None)
        for g in reversed(psg):
            g.__exit__(None, None, None)
        for g in reversed(guards):
            g.__exit__(None, None, None)

    return nc



def _weight_images(inputs):
    qw = np.asarray(inputs["L0_attn_q_w"], np.float32)
    kw = np.asarray(inputs["L0_attn_k_w"], np.float32)
    vw = np.asarray(inputs["L0_attn_v_w"], np.float32)
    ow = np.asarray(inputs["L0_attn_o_w"], np.float32)
    gu = np.asarray(inputs["L0_ffn_w_gate_up"], np.float32)
    dw = np.asarray(inputs["L0_ffn_w_down"], np.float32)

    def fm_img(wmat):
        return np.ascontiguousarray(
            wmat.reshape(16, 128, 16, 128).transpose(0, 3, 2, 1).reshape(16, 128, 2048))
    wq_img = fm_img(qw)
    wk_img = fm_img(kw)
    wo_img = fm_img(ow)
    wv_img = np.ascontiguousarray(vw.T.reshape(16, 128, 2048))
    wgu_img = np.ascontiguousarray(
        gu.reshape(128, 128, 16, 128).transpose(0, 3, 2, 1).reshape(128, 128, 2048))
    wd_img = np.ascontiguousarray(
        dw.reshape(16, 128, 64, 128).transpose(0, 3, 2, 1).reshape(16, 128, 8192))
    return wq_img, wk_img, wv_img, wo_img, wgu_img, wd_img


def _core_masks(r):
    ki = np.arange(128)[:, None]
    qi = np.arange(128)[None, :]
    m = np.zeros((4, 128, 128), dtype=np.float32)
    for kc in range(4):
        m[kc] = (128 * kc + ki <= 128 * r + qi).astype(np.float32)
    return np.ascontiguousarray(m.transpose(1, 0, 2)).astype(ml_dtypes.bfloat16)


_nc_cache = {}


def kernel(**inputs):
    key = "prog_dbg" if DEBUG else "prog"
    if key not in _nc_cache:
        _nc_cache[key] = build_program()
    nc = _nc_cache[key]

    x = np.asarray(inputs["L0_attn_hidden_states"], dtype=np.float32)
    wq_img, wk_img, wv_img, wo_img, wgu_img, wd_img = _weight_images(inputs)
    qb_h = np.ascontiguousarray(np.asarray(inputs["L0_attn_q_b"], np.float32).reshape(16, 128).T)
    kb_h = np.ascontiguousarray(np.asarray(inputs["L0_attn_k_b"], np.float32).reshape(16, 128).T)
    vb_h = np.ascontiguousarray(np.asarray(inputs["L0_attn_v_b"], np.float32).reshape(1, 2048))
    ob_h = np.ascontiguousarray(np.asarray(inputs["L0_attn_o_b"], np.float32).reshape(16, 128).T)

    in_maps = []
    idx_of_core = []
    for c in range(8):
        b, r = c // 4, c % 4
        idx = np.concatenate([np.arange(512 * g + 128 * r, 512 * g + 128 * r + 128)
                              for g in range(8)])
        idx_of_core.append((b, idx))
        xT_c = np.ascontiguousarray(x[b].T[:, idx])
        in_maps.append({
            "xT": xT_c,
            "wq": wq_img, "wk": wk_img, "wv": wv_img, "wo": wo_img,
            "wgu": wgu_img, "wd": wd_img,
            "qb": qb_h, "kb": kb_h, "vb": vb_h, "ob": ob_h,
            "masks": _core_masks(r),
        })

    res = run_bass_kernel_spmd(nc, in_maps, core_ids=list(range(8)))

    out = np.empty((NB, S, D), dtype=np.float32)
    for c in range(8):
        b, idx = idx_of_core[c]
        out[b, idx, :] = np.asarray(res.results[c]["outT"]).T
    if DEBUG:
        kernel.debug_results = res.results
    return out
